# version 03
# baseline (speedup 1.0000x reference)
import numpy as np
import ml_dtypes
import contextlib
import concourse.bass as bass
import concourse.mybir as mybir
from concourse.bass_utils import run_bass_kernel_spmd

F32 = mybir.dt.float32
BF16 = mybir.dt.bfloat16
I32 = mybir.dt.int32
AF = mybir.ActivationFunctionType
ALU = mybir.AluOpType
AX = mybir.AxisListType

B, L, D = 2, 8192, 4096
NCORES = 8
NS = 16
EPS_LN = 1e-5
EPS_R = 1e-12
SCALING = 2.0
TOK = (B * L) // NCORES
DQ = D // 4
LC = 512
NCH = L // LC
NT = DQ // 128
KD = D // 128


class Sched:
    def __init__(self, engines):
        self.ops = {e: [] for e in engines}
        self.guar = {e: {} for e in engines}

    def emit(self, eng, fn):
        self.ops[eng].append(fn)

    def wait(self, eng, key, sem, val):
        if val is None or val <= 0:
            return
        if val <= self.guar[eng].get(key, 0):
            return
        self.guar[eng][key] = val
        self.ops[eng].append(lambda e, s=sem, v=val: e.wait_ge(s, v))

    def replay(self, eng, handle):
        for fn in self.ops[eng]:
            fn(handle)


def _mk_emitters(sc, sems):
    state = {"act": 0, "dve": 0, "pe": 0, "din": 0}

    def dma(out, in_, sp=False):
        state["din"] += 2
        sc.emit("sync", lambda e, o=out, i=in_, s_=sp: e.dma_start(out=o, in_=i, single_packet=s_).then_inc(sems["d_in"], 16))
        sc.emit("sync", lambda e, o=out, i=in_, s_=sp: e.dma_start(out=o, in_=i, single_packet=s_).then_inc(sems["d_in"], 16))

    def act(fn):
        state["act"] += 1
        sc.emit("scalar", lambda e, f=fn: f(e).then_inc(sems["s_act"], 1))

    def dve(fn):
        state["dve"] += 1
        sc.emit("vector", lambda e, f=fn: f(e).then_inc(sems["s_dve"], 1))

    def pe(fn):
        state["pe"] += 1
        sc.emit("tensor", lambda e, f=fn: f(e).then_inc(sems["s_pe"], 1))

    return dma, act, dve, pe, state



def build_l0():
    nc = bass.Bass("TRN2", target_bir_lowering=False)
    xr_h = nc.dram_tensor("xrows", [TOK, D], F32, kind="ExternalInput").ap()
    wp_h = nc.dram_tensor("xpwT", [D, 160], F32, kind="ExternalInput").ap()
    idb_h = nc.dram_tensor("identf", [128, 128], F32, kind="ExternalInput").ap()
    st_h = nc.dram_tensor("stats", [128, TOK // 128, 2], F32, kind="ExternalOutput").ap()
    hi_h = nc.dram_tensor("xp_hi", [128, TOK], F32, kind="ExternalOutput").ap()
    lo_h = nc.dram_tensor("xp_lo", [32, TOK], F32, kind="ExternalOutput").ap()
    NTL = TOK // 128

    with contextlib.ExitStack() as ctx:
        sbt = lambda n, s, d: ctx.enter_context(nc.sbuf_tensor(n, s, d))
        pst = lambda n, s, d: ctx.enter_context(nc.psum_tensor(n, s, d))
        sem = lambda n: ctx.enter_context(nc.semaphore(n))

        xrow = sbt("xrow", [128, 2, D], F32)
        sq = sbt("sq", [128, D], F32)
        xlnb = sbt("xlnb", [128, 2, D], BF16)
        xlnT = sbt("xlnT", [128, KD, 128], BF16)
        wp = sbt("wp", [128, KD, 160], F32)
        wpb = sbt("wpb", [128, KD, 160], BF16)
        identb = sbt("ident", [128, 128], BF16)
        identf_stg = sbt("identf_stg", [128, 128], F32)
        stats = sbt("stats_sb", [128, NTL, 2], F32)
        tc = sbt("tc", [128, 4], F32)
        xph = sbt("xph", [128, NTL, 128], F32)
        xpl = sbt("xpl", [32, NTL, 128], F32)
        ptr = [pst(f"ptr{i}", [128, 128], BF16) for i in range(2)]
        pxh = pst("pxh", [128, 128], F32)
        pxl = pst("pxl", [32, 128], F32)

        sems = dict(d_in=sem("d_in"), s_act=sem("s_act"), s_dve=sem("s_dve"), s_pe=sem("s_pe"))
        s_out = sem("s_out")
        sc = Sched(["sync", "vector", "scalar", "tensor"])
        dma, act, dve, pe, st = _mk_emitters(sc, sems)
        d_in, s_act, s_dve, s_pe = sems["d_in"], sems["s_act"], sems["s_dve"], sems["s_pe"]

        for kc in range(KD):
            dma(wp[:, kc, :], wp_h.rearrange("(c p) m -> p c m", p=128)[:, kc, :])
        dma(identf_stg[:], idb_h)
        sc.wait("vector", "di", d_in, 16 * st["din"])
        dve(lambda e: e.tensor_copy(out=wpb[:], in_=wp[:]))
        dve(lambda e: e.tensor_copy(out=identb[:], in_=identf_stg[:]))

        ln_act, evac_dve, trans_pe, mmev_dve = {}, {}, {}, {}

        for i in range(NTL):
            par = i % 2
            if i >= 2:
                sc.wait("sync", "v", s_dve, ln_act[i - 2])
            dma(xrow[:, par, :], xr_h.rearrange("(t p) d -> t p d", p=128)[i])
            c0, c1, c2 = tc[:, 0:1], tc[:, 1:2], tc[:, 2:3]
            mu, rstd = stats[:, i, 0:1], stats[:, i, 1:2]

            sc.wait("vector", "di", d_in, 16 * st["din"])
            dve(lambda e, p=par, c=c0: e.reduce_sum(out=c, in_=xrow[:, p, :], axis=AX.X))
            sc.wait("vector", "v", s_dve, st["dve"])
            dve(lambda e, c=c0, m=mu: e.tensor_scalar_mul(out=m, in0=c, scalar1=1.0 / D))
            sc.wait("scalar", "di", d_in, 16 * st["din"])
            act(lambda e, p=par: e.activation(out=sq[:], in_=xrow[:, p, :], func=AF.Square))
            sc.wait("vector", "a", s_act, st["act"])
            dve(lambda e, c=c1: e.reduce_sum(out=c, in_=sq[:], axis=AX.X))
            sc.wait("vector", "v", s_dve, st["dve"])
            dve(lambda e, c=c1: e.tensor_scalar_mul(out=c, in0=c, scalar1=1.0 / D))
            sc.wait("vector", "v", s_dve, st["dve"])
            dve(lambda e, m=mu, c=c2: e.tensor_mul(c, m, m))
            sc.wait("vector", "v", s_dve, st["dve"])
            dve(lambda e, a=c1, b_=c2: e.tensor_sub(a, a, b_))
            sc.wait("vector", "v", s_dve, st["dve"])
            dve(lambda e, c=c1: e.tensor_scalar_add(out=c, in0=c, scalar1=EPS_LN))
            sc.wait("scalar", "v", s_dve, st["dve"])
            act(lambda e, c=c1: e.activation(out=c, in_=c, func=AF.Ln))
            sc.wait("vector", "a", s_act, st["act"])
            dve(lambda e, c=c1: e.tensor_scalar_mul(out=c, in0=c, scalar1=-0.5))
            sc.wait("scalar", "v", s_dve, st["dve"])
            act(lambda e, c=c1, r=rstd: e.activation(out=r, in_=c, func=AF.Exp))
            sc.wait("vector", "a", s_act, st["act"])
            dve(lambda e, m=mu, r=rstd, c=c2: e.tensor_mul(c, m, r))
            sc.wait("vector", "v", s_dve, st["dve"])
            dve(lambda e, c=c2: e.tensor_scalar_mul(out=c, in0=c, scalar1=-1.0))
            sc.wait("vector", "v", s_dve, st["dve"])
            if i >= 2:
                sc.wait("vector", "p", s_pe, trans_pe[i - 2])
            dve(lambda e, p=par, r=rstd, c=c2: e.tensor_scalar(out=xlnb[:, p, :], in0=xrow[:, p, :], scalar1=r, scalar2=c, op0=ALU.mult, op1=ALU.add))
            ln_act[i] = st["dve"]

            for j in range(KD):
                pj = j % 2
                sc.wait("tensor", "v", s_dve, ln_act[i])
                prev = evac_dve.get((i, j - 2)) if j >= 2 else (evac_dve.get((i - 1, KD - 2 + j)) if i >= 1 else None)
                sc.wait("tensor", "v", s_dve, prev)
                pe(lambda e, p=par, j_=j, pj_=pj: e.transpose(ptr[pj_][:], xlnb[:, p, j_ * 128:(j_ + 1) * 128], identb[:]))
                sc.wait("vector", "p", s_pe, st["pe"])
                dve(lambda e, j_=j, pj_=pj: e.tensor_copy(out=xlnT[:, j_, :], in_=ptr[pj_][:]))
                evac_dve[(i, j)] = st["dve"]
            trans_pe[i] = st["pe"]

            sc.wait("tensor", "v", s_dve, evac_dve[(i, KD - 1)])
            if i >= 1:
                sc.wait("tensor", "v", s_dve, mmev_dve[i - 1])
            for kc in range(KD):
                pe(lambda e, k=kc: e.matmul(pxh[:], wpb[:, k, 0:128], xlnT[:, k, :], start=(k == 0), stop=(k == KD - 1)))
            for kc in range(KD):
                pe(lambda e, k=kc: e.matmul(pxl[:], wpb[:, k, 128:160], xlnT[:, k, :], start=(k == 0), stop=(k == KD - 1)))
            sc.wait("vector", "p", s_pe, st["pe"])
            dve(lambda e, i_=i: e.tensor_copy(out=xph[:, i_, :], in_=pxh[:]))
            dve(lambda e, i_=i: e.tensor_copy(out=xpl[:, i_, :], in_=pxl[:, :]))
            mmev_dve[i] = st["dve"]

        sc.wait("sync", "v", s_dve, st["dve"])
        sc.wait("sync", "a", s_act, st["act"])
        sc.emit("sync", lambda e: e.dma_start(out=st_h, in_=stats[:]).then_inc(s_out, 16))
        sc.emit("sync", lambda e: e.dma_start(out=hi_h.rearrange("p (t l) -> p t l", l=128), in_=xph[:]).then_inc(s_out, 16))
        sc.emit("sync", lambda e: e.dma_start(out=lo_h.rearrange("p (t l) -> p t l", l=128), in_=xpl[:]).then_inc(s_out, 16))
        sc.emit("sync", lambda e: e.wait_ge(s_out, 48))

        with nc.Block() as block:
            @block.sync
            def _(eng):
                sc.replay("sync", eng)

            @block.vector
            def _(eng):
                sc.replay("vector", eng)

            @block.scalar
            def _(eng):
                sc.replay("scalar", eng)

            @block.tensor
            def _(eng):
                sc.replay("tensor", eng)
    return nc



def build_l1(nch=NCH, debug=False):
    nc = bass.Bass("TRN2", target_bir_lowering=False)
    xqT_h = nc.dram_tensor("xqT", [DQ, L], F32, kind="ExternalInput").ap()
    stT_h = nc.dram_tensor("statsT", [2, L], F32, kind="ExternalInput").ap()
    xpT_h = nc.dram_tensor("xpT", [160, L], F32, kind="ExternalInput").ap()
    dtw_h = nc.dram_tensor("dtwT", [128, DQ], F32, kind="ExternalInput").ap()
    dtb_h = nc.dram_tensor("dtb", [128, NT], F32, kind="ExternalInput").ap()
    alog_h = nc.dram_tensor("alog", [128, NT, NS], F32, kind="ExternalInput").ap()
    dvec_h = nc.dram_tensor("dvec", [128, NT], F32, kind="ExternalInput").ap()
    ya_h = nc.dram_tensor("ya", [DQ, L], F32, kind="ExternalOutput").ap()
    if debug == 2:
        dm2 = {nm: nc.dram_tensor("d2_" + nm, [128, 4, 128], F32, kind="ExternalOutput").ap()
               for nm in ["dA", "ub", "hb"]}
        for nm in ["dt", "xl", "mu", "rs", "xq"]:
            dm2[nm] = nc.dram_tensor("d2_" + nm, [128, 128], F32, kind="ExternalOutput").ap()
    if debug:
        dmp = {nm: nc.dram_tensor("dmp_" + nm, shp, F32, kind="ExternalOutput").ap()
               for nm, shp in [("xln", [128, NT, LC]), ("dtT", [128, NT, LC]), ("dtx", [128, NT, LC]),
                               ("Bbc", [128, 4, LC]), ("Cbc", [128, 4, LC]), ("dA", [128, 4, LC]),
                               ("ub", [128, 4, LC]), ("hb", [128, 4, LC])]}

    with contextlib.ExitStack() as ctx:
        sbt = lambda n, s, d: ctx.enter_context(nc.sbuf_tensor(n, s, d))
        pst = lambda n, s, d: ctx.enter_context(nc.psum_tensor(n, s, d))
        sem = lambda n: ctx.enter_context(nc.semaphore(n))

        xq = sbt("xq", [128, NT, LC], F32)
        xln = sbt("xln", [128, NT, LC], F32)
        dtT = sbt("dtT", [128, NT, LC], F32)
        dtx = sbt("dtx", [128, NT, LC], F32)
        Bbc = sbt("Bbc", [128, 2, 4, LC], F32)
        Cbc = sbt("Cbc", [128, 2, 4, LC], F32)
        dA = sbt("dA", [128, 4, 4, LC], F32)
        ub = sbt("ub", [128, 4, LC], F32)
        hb = sbt("hb", [128, 4, LC], F32)
        yb = sbt("yb", [128, NT, LC], F32)
        mubc = sbt("mubc", [128, 2, LC], F32)
        rsbc = sbt("rsbc", [128, 2, LC], F32)
        dtin = sbt("dtin", [128, 2, LC], F32)
        dtinb = sbt("dtinb", [128, 2, LC], BF16)
        esb = sbt("esb", [128, LC], F32)
        dtw = sbt("dtw_sb", [128, DQ], F32)
        dtwb = sbt("dtwb", [128, DQ], BF16)
        alog = sbt("alog_sb", [128, NT, NS], F32)
        Asb = sbt("Asb", [128, NT, NS], F32)
        dtb = sbt("dtb_sb", [128, NT], F32)
        dvec = sbt("dvec_sb", [128, NT], F32)
        carry = sbt("carry", [128, NT, NS], F32)
        pdt = [pst(f"pdt{i}", [128, LC], F32) for i in range(2)]
        if debug == 2:
            db2 = {nm: sbt("db2_" + nm, [128, 4, 128], F32) for nm in ["dA", "ub", "hb"]}
            for nm in ["dt", "xl", "mu", "rs", "xq"]:
                db2[nm] = sbt("db2_" + nm, [128, 128], F32)

        sems = dict(d_in=sem("d_in"), s_act=sem("s_act"), s_dve=sem("s_dve"), s_pe=sem("s_pe"))
        s_out = sem("s_out")
        d_rdy = sem("d_rdy")
        sc = Sched(["sync", "vector", "scalar", "tensor"])
        dma, act, dve, pe, st = _mk_emitters(sc, sems)
        d_in, s_act, s_dve, s_pe = sems["d_in"], sems["s_act"], sems["s_dve"], sems["s_pe"]
        n_out = [0]
        n_rdy = [0]

        def dbarrier():
            return st["din"]

        def dma_out(out, in_):
            n_out[0] += 1
            sc.emit("sync", lambda e, o=out, i=in_: e.dma_start(out=o, in_=i).then_inc(s_out, 16))

        def dump(name, src_ap):
            if not debug:
                return
            sc.wait("sync", "v", s_dve, st["dve"])
            sc.wait("sync", "a", s_act, st["act"])
            dma_out(dmp[name], src_ap)
            sc.wait("vector", "o", s_out, 16 * n_out[0])
            sc.wait("scalar", "o", s_out, 16 * n_out[0])

        dma(dtw[:], dtw_h)
        dma(dtb[:], dtb_h)
        dma(alog[:], alog_h)
        dma(dvec[:], dvec_h)
        rdy0 = dbarrier()
        sc.wait("vector", "di", d_in, 16 * rdy0)
        dve(lambda e: e.tensor_copy(out=dtwb[:], in_=dtw[:]))
        dve(lambda e: e.memset(carry[:], 0.0))
        sc.wait("scalar", "di", d_in, 16 * rdy0)
        act(lambda e: e.activation(out=Asb[:], in_=alog[:], func=AF.Exp))
        sc.wait("vector", "a", s_act, st["act"])
        dve(lambda e: e.tensor_scalar_mul(out=Asb[:], in0=Asb[:], scalar1=-1.0))
        asb_done = st["dve"]

        ln_done, cast_done, dtx_done = {}, {}, {}
        bc_used, da_used, exp_act = {}, {}, {}
        mm_pe, ya_done, out_cnt = {}, {}, {}
        din_after = {}

        bcdin = {}

        def emit_bc(G):
            if G >= nch * 4:
                return
            cg, g4 = G // 4, G % 4
            sc.wait("sync", "v", s_dve, bc_used.get(G - 2))
            for k in range(4):
                n = g4 * 4 + k
                dma(Bbc[:, G % 2, k, :], xpT_h[128 + n:129 + n, cg * LC:(cg + 1) * LC].partition_broadcast(128))
                dma(Cbc[:, G % 2, k, :], xpT_h[144 + n:145 + n, cg * LC:(cg + 1) * LC].partition_broadcast(128))
            bcdin[G] = dbarrier()

        def emit_loads(c):
            par = c % 2
            sc.wait("sync", "v", s_dve, ln_done.get(c - 1))
            for tq in range(NT):
                dma(xq[:, tq, :], xqT_h.rearrange("(t p) l -> p t l", p=128)[:, tq, c * LC:(c + 1) * LC])
            sc.wait("sync", "v", s_dve, cast_done.get(c - 2))
            dma(dtin[:, par, :], xpT_h[0:128, c * LC:(c + 1) * LC])
            dma(mubc[:, par, :], stT_h[0:1, c * LC:(c + 1) * LC].partition_broadcast(128))
            dma(rsbc[:, par, :], stT_h[1:2, c * LC:(c + 1) * LC].partition_broadcast(128))
            din_after[c] = dbarrier()

        emit_loads(0)
        emit_bc(0)
        emit_bc(1)

        for c in range(nch):
            par = c % 2
            sc.wait("vector", "di", d_in, 16 * din_after[c])
            sc.wait("vector", "p", s_pe, mm_pe.get(c - 2))
            dve(lambda e, p=par: e.tensor_copy(out=dtinb[:, p, :], in_=dtin[:, p, :]))
            cast_done[c] = st["dve"]
            for t in range(NT):
                dve(lambda e, t_=t, p=par: e.tensor_sub(xln[:, t_, :], xq[:, t_, :], mubc[:, p, :]))
                sc.wait("vector", "v", s_dve, st["dve"])
                dve(lambda e, t_=t, p=par: e.tensor_mul(xln[:, t_, :], xln[:, t_, :], rsbc[:, p, :]))
                sc.wait("vector", "v", s_dve, st["dve"])
            if debug == 2 and c == 0:
                dve(lambda e: e.tensor_copy(out=db2["xq"][:], in_=xq[:, 0, 384:512]))
                dve(lambda e: e.tensor_copy(out=db2["mu"][:], in_=mubc[:, 0, 384:512]))
                dve(lambda e: e.tensor_copy(out=db2["rs"][:], in_=rsbc[:, 0, 384:512]))
                dve(lambda e: e.tensor_copy(out=db2["xl"][:], in_=xln[:, 0, 384:512]))
            ln_done[c] = st["dve"]
            if c == 0:
                dump("xln", xln[:])
            if c + 1 < nch:
                emit_loads(c + 1)

            for t in range(NT):
                sc.wait("tensor", "v", s_dve, cast_done[c])
                prev_exp = exp_act.get((c, t - 2)) if t >= 2 else exp_act.get((c - 1, NT - 2 + t))
                sc.wait("tensor", "a", s_act, prev_exp)
                pe(lambda e, t_=t, p=par: e.matmul(pdt[t_ % 2][:], dtwb[:, t_ * 128:(t_ + 1) * 128], dtinb[:, p, :], start=True, stop=True))
                sc.wait("scalar", "p", s_pe, st["pe"])
                sc.wait("scalar", "v", s_dve, dtx_done.get(c - 1))
                act(lambda e, t_=t: e.activation(out=esb[:], in_=pdt[t_ % 2][:], func=AF.Exp, bias=dtb[:, t_:t_ + 1]))
                exp_act[(c, t)] = st["act"]
                act(lambda e, t_=t: e.activation(out=dtT[:, t_, :], in_=esb[:], func=AF.Ln, bias=1.0))
            mm_pe[c] = st["pe"]
            sp_done = st["act"]
            if c == 0:
                dump("dtT", dtT[:])

            sc.wait("vector", "a", s_act, sp_done)
            for t in range(NT):
                dve(lambda e, t_=t: e.tensor_mul(dtx[:, t_, :], dtT[:, t_, :], xln[:, t_, :]))
            dtx_done[c] = st["dve"]
            if c == 0:
                dump("dtx", dtx[:])

            for g4 in range(4):
                G = c * 4 + g4
                for t in range(NT):
                    gidx = (G * NT + t) % 4
                    sc.wait("scalar", "v", s_dve, max(asb_done, da_used.get(gidx) or 0))
                    for k in range(4):
                        n = g4 * 4 + k
                        act(lambda e, t_=t, k_=k, n_=n, s=gidx: e.activation(
                            out=dA[:, s, k_, :], in_=dtT[:, t_, :], func=AF.Exp, scale=Asb[:, t_, n_:n_ + 1]))
                    da_act = st["act"]
                    sc.wait("vector", "di", d_in, 16 * bcdin[G])
                    sc.wait("vector", "a", s_act, da_act)
                    if debug and G == 0 and t == 0:
                        dump("Bbc", Bbc[:, 0, :, :])
                        dump("Cbc", Cbc[:, 0, :, :])
                        dump("dA", dA[:, 0, :, :])
                    dve(lambda e, t_=t, g=G: e.tensor_mul(
                        ub[:], dtx[:, t_, :][:, None, :].broadcast_to([128, 4, LC]), Bbc[:, g % 2, :, :]))
                    if debug and G == 0 and t == 0:
                        dump("ub", ub[:])
                    for k in range(4):
                        n = g4 * 4 + k
                        dve(lambda e, t_=t, k_=k, n_=n, s=gidx: e.tensor_tensor_scan(
                            out=hb[:, k_, :], data0=dA[:, s, k_, :], data1=ub[:, k_, :],
                            initial=carry[:, t_, n_:n_ + 1], op0=ALU.mult, op1=ALU.add))
                    if debug == 2 and G == 0 and t == 0:
                        dve(lambda e, s=gidx: e.tensor_copy(out=db2["dA"][:], in_=dA[:, s, :, 384:512]))
                        dve(lambda e: e.tensor_copy(out=db2["ub"][:], in_=ub[:, :, 384:512]))
                        dve(lambda e: e.tensor_copy(out=db2["hb"][:], in_=hb[:, :, 384:512]))
                        dve(lambda e: e.tensor_copy(out=db2["dt"][:], in_=dtT[:, 0, 384:512]))

                    da_used[gidx] = st["dve"]
                    if debug and G == 0 and t == 0:
                        dump("hb", hb[:])
                    sc.wait("vector", "v", s_dve, st["dve"])
                    dve(lambda e, t_=t, g4_=g4: e.tensor_copy(
                        out=carry[:, t_, g4_ * 4:(g4_ + 1) * 4], in_=hb[:, :, LC - 1]))
                    sc.wait("vector", "v", s_dve, st["dve"])
                    dve(lambda e, g=G: e.tensor_mul(hb[:], hb[:], Cbc[:, g % 2, :, :]))
                    dve(lambda e: e.tensor_add(hb[:, 0:2, :], hb[:, 0:2, :], hb[:, 2:4, :]))
                    if g4 == 0:
                        oc = out_cnt.get((c - 1, t))
                        sc.wait("vector", "o", s_out, 16 * oc if oc else None)
                        dve(lambda e, t_=t: e.tensor_add(yb[:, t_, :], hb[:, 0, :], hb[:, 1, :]))
                    else:
                        dve(lambda e: e.tensor_add(hb[:, 0, :], hb[:, 0, :], hb[:, 1, :]))
                        dve(lambda e, t_=t: e.tensor_add(yb[:, t_, :], yb[:, t_, :], hb[:, 0, :]))
                bc_used[G] = st["dve"]
                emit_bc(G + 2)

            for t in range(NT):
                dve(lambda e, t_=t: e.scalar_tensor_tensor(
                    out=yb[:, t_, :], in0=xln[:, t_, :], scalar=dvec[:, t_:t_ + 1], in1=yb[:, t_, :],
                    op0=ALU.mult, op1=ALU.add))
                ya_done[(c, t)] = st["dve"]
                sc.wait("sync", "v", s_dve, ya_done[(c, t)])
                dma_out(ya_h[t * 128:(t + 1) * 128, c * LC:(c + 1) * LC], yb[:, t, :])
                out_cnt[(c, t)] = n_out[0]

        if debug == 2:
            sc.wait("sync", "v", s_dve, st["dve"])
            for nm in ["dA", "ub", "hb", "dt", "xl", "mu", "rs", "xq"]:
                dma_out(dm2[nm], db2[nm][:])
        sc.emit("sync", lambda e: e.wait_ge(s_out, 16 * n_out[0]))

        with nc.Block() as block:
            @block.sync
            def _(eng):
                sc.replay("sync", eng)

            @block.vector
            def _(eng):
                sc.replay("vector", eng)

            @block.scalar
            def _(eng):
                sc.replay("scalar", eng)

            @block.tensor
            def _(eng):
                sc.replay("tensor", eng)
    return nc



def build_l2(phases=3, debug=False):
    nc = bass.Bass("TRN2", target_bir_lowering=False)
    xr_h = nc.dram_tensor("xrows", [TOK, D], F32, kind="ExternalInput").ap()
    yr_h = nc.dram_tensor("yarows", [TOK, D], F32, kind="ExternalInput").ap()
    wt_h = nc.dram_tensor("WTb", [32, 128, KD, 128], F32, kind="ExternalInput").ap()
    bb_h = nc.dram_tensor("bb", [128, 32], F32, kind="ExternalInput").ap()
    lA_h = nc.dram_tensor("loraAT", [D, 128], F32, kind="ExternalInput").ap()
    lB_h = nc.dram_tensor("loraBT", [128, D], F32, kind="ExternalInput").ap()
    rw_h = nc.dram_tensor("rwT", [D, 16], F32, kind="ExternalInput").ap()
    rb_h = nc.dram_tensor("rb", [16, 1], F32, kind="ExternalInput").ap()
    pr_h = nc.dram_tensor("prior", [16, 1], F32, kind="ExternalInput").ap()
    ac_h = nc.dram_tensor("act4", [4, 1], I32, kind="ExternalInput").ap()
    acr_h = nc.dram_tensor("actrow", [1, 4], I32, kind="ExternalInput").ap()
    io16_h = nc.dram_tensor("iota16", [16, 1], F32, kind="ExternalInput").ap()
    ior_h = nc.dram_tensor("iotar", [1, 16], F32, kind="ExternalInput").ap()
    idf_h = nc.dram_tensor("identf", [128, 128], F32, kind="ExternalInput").ap()
    yo_h = nc.dram_tensor("youtT", [D, TOK], F32, kind="ExternalOutput").ap()
    x1sc = nc.dram_tensor("x1T_sc", [D, TOK], F32).ap()
    gate_sc = nc.dram_tensor("gate_sc", [16, TOK // 2], F32).ap()
    if debug:
        dmp = {nm: nc.dram_tensor("dmp_" + nm, shp, F32, kind="ExternalOutput").ap()
               for nm, shp in [("x1", [128, D]), ("xt1", [128, D]), ("zall", [128, TOK // 2]),
                               ("logits", [16, TOK // 2]), ("lslot", [4, TOK // 2]),
                               ("atok", [128, (TOK // 2) // 128, 4]), ("gateT", [16, TOK // 2]),
                               ("wl", [128, TOK // 2]), ("gbc", [128, TOK // 2])]}

    HT = TOK // 2
    NTL = HT // 128

    with contextlib.ExitStack() as ctx:
        sbt = lambda n, s, d: ctx.enter_context(nc.sbuf_tensor(n, s, d))
        pst = lambda n, s, d: ctx.enter_context(nc.psum_tensor(n, s, d))
        sem = lambda n: ctx.enter_context(nc.semaphore(n))

        xrow = sbt("xrow", [128, D], F32)
        yarow = sbt("yarow", [128, D], F32)
        x1 = sbt("x1", [128, D], F32)
        xt1b = sbt("xt1b", [128, D], BF16)
        x1stg = sbt("x1stg", [128, 2, 128], F32)
        xT = sbt("xT", [128, KD, HT], BF16)
        wblk = sbt("wblk", [128, KD, 128], F32)
        wblkb = sbt("wblkb", [128, 2, KD, 128], BF16)
        stgf = sbt("stgf", [128, 128], F32)
        bbs = sbt("bbs", [128, 32], F32)
        lATb = sbt("lATb", [128, KD, 128], BF16)
        lBTb = sbt("lBTb", [128, D], BF16)
        rwTb = sbt("rwTb", [128, KD, 16], BF16)
        rb = sbt("rb_sb", [16, 1], F32)
        prior = sbt("prior_sb", [16, 1], F32)
        act4 = sbt("act4_sb", [4, 1], I32)
        actf = sbt("actf", [4, 1], F32)
        acri = sbt("acri", [1, 4], I32)
        actrow = sbt("actrow_sb", [1, 4], F32)
        ones16 = sbt("ones16", [1, 16], F32)
        io16 = sbt("io16_sb", [16, 1], F32)
        iorow = sbt("iorow", [4, 16], F32)
        oh164 = sbt("oh164", [16, 4], F32)
        oh46 = sbt("oh46", [4, 16], F32)
        acbc = sbt("acbc", [16, 4], F32)
        pslots = sbt("pslots", [4, 1], F32)
        sbias = sbt("sbias", [4, 1], F32)
        identb = sbt("identb_sb", [128, 128], BF16)
        identf = sbt("identf_sb", [128, 128], F32)
        tc = sbt("tc", [128, 4], F32)
        zall = sbt("zall", [128, HT], F32)
        gatebc = sbt("gatebc", [128, HT], F32)
        wlorab = sbt("wlorab", [128, HT], BF16)
        logits = sbt("logits", [16, HT], F32)
        lslot = sbt("lslot", [4, HT], F32)
        ltok = sbt("ltok", [128, NTL, 4], F32)
        atok = sbt("atok", [128, NTL, 4], F32)
        aslT = sbt("aslT", [4, HT], F32)
        gateT = sbt("gateT", [16, HT], F32)
        x1t = sbt("x1t", [128, 2, 512], F32)
        yo = sbt("yo", [128, 2, 512], F32)

        ptrb = [pst(f"ptrb{i}", [128, 128], BF16) for i in range(2)]
        ptrf = [pst(f"ptrf{i}", [128, 128], F32) for i in range(2)]
        pw = [pst(f"pw{i}", [128, 512], F32) for i in range(2)]
        pz = pst("pz", [128, 512], F32)
        plog = pst("plog", [16, 512], F32)

        sems = dict(d_in=sem("d_in"), s_act=sem("s_act"), s_dve=sem("s_dve"), s_pe=sem("s_pe"))
        d_w = sem("d_w")
        s_out = sem("s_out")
        s_x1 = sem("s_x1")
        sc = Sched(["sync", "vector", "scalar", "tensor"])
        dma, act, dve, pe, st = _mk_emitters(sc, sems)
        d_in, s_act, s_dve, s_pe = sems["d_in"], sems["s_act"], sems["s_dve"], sems["s_pe"]
        n_dw = [0]
        n_out = [0]
        n_x1 = [0]

        def dma_w(out, in_):
            n_dw[0] += 1
            sc.emit("scalar", lambda e, o=out, i=in_: e.dma_start(out=o, in_=i).then_inc(d_w, 16))

        def dma_out(out, in_):
            n_out[0] += 1
            sc.emit("sync", lambda e, o=out, i=in_: e.dma_start(out=o, in_=i).then_inc(s_out, 16))

        def dma_x1(out, in_):
            n_x1[0] += 1
            sc.emit("sync", lambda e, o=out, i=in_: e.dma_start(out=o, in_=i).then_inc(s_x1, 16))

        def dump(name, src_ap):
            if not debug:
                return
            sc.wait("sync", "v", s_dve, st["dve"])
            sc.wait("sync", "a", s_act, st["act"])
            dma_out(dmp[name], src_ap)
            sc.wait("vector", "o", s_out, 16 * n_out[0])
            sc.wait("scalar", "o", s_out, 16 * n_out[0])

        dma(bbs[:], bb_h)
        dma(rb[:], rb_h)
        dma(prior[:], pr_h)
        dma(act4[:], ac_h)
        dma(acri[:], acr_h)
        dma(io16[:], io16_h)
        dma(identf[:], idf_h)
        dma(iorow[:], ior_h.partition_broadcast(4))
        cast_waits = []
        for jc in range(KD):
            dma(stgf[:], lA_h.rearrange("(c p) m -> p c m", p=128)[:, jc, :])
            sc.wait("vector", "di", d_in, 16 * st["din"])
            dve(lambda e, j=jc: e.tensor_copy(out=lATb[:, j, :], in_=stgf[:]))
            cast_waits.append(st["dve"])
            sc.wait("sync", "v", s_dve, st["dve"])
        for jc in range(KD):
            dma(stgf[:], lB_h[:, jc * 128:(jc + 1) * 128])
            sc.wait("vector", "di", d_in, 16 * st["din"])
            dve(lambda e, j=jc: e.tensor_copy(out=lBTb[:, j * 128:(j + 1) * 128], in_=stgf[:, :]))
            sc.wait("sync", "v", s_dve, st["dve"])
        for jc in range(KD):
            dma(stgf[:, 0:16], rw_h.rearrange("(c p) m -> p c m", p=128)[:, jc, :])
            sc.wait("vector", "di", d_in, 16 * st["din"])
            dve(lambda e, j=jc: e.tensor_copy(out=rwTb[:, j, :], in_=stgf[:, 0:16]))
            sc.wait("sync", "v", s_dve, st["dve"])

        sc.wait("vector", "di", d_in, 16 * st["din"])
        dve(lambda e: e.tensor_copy(out=identb[:], in_=identf[:]))
        dve(lambda e: e.tensor_copy(out=actf[:], in_=act4[:]))
        dve(lambda e: e.tensor_copy(out=actrow[:], in_=acri[:]))
        dve(lambda e: e.memset(ones16[:], 1.0))
        sc.wait("vector", "v", s_dve, st["dve"])
        sc.wait("tensor", "v", s_dve, st["dve"])
        pe(lambda e: e.matmul(plog[0:16, 0:4], ones16[:], actrow[:], start=True, stop=True))
        sc.wait("vector", "p", s_pe, st["pe"])
        dve(lambda e: e.tensor_copy(out=acbc[:], in_=plog[0:16, 0:4]))
        sc.wait("vector", "v", s_dve, st["dve"])
        dve(lambda e: e.tensor_tensor(out=oh164[:], in0=io16.broadcast_to([16, 4]), in1=acbc[:], op=ALU.is_equal))
        sc.wait("vector", "v", s_dve, st["dve"])
        dve(lambda e: e.tensor_tensor(out=oh46[:], in0=actf.broadcast_to([4, 16]), in1=iorow[:], op=ALU.is_equal))
        sc.wait("vector", "v", s_dve, st["dve"])
        sc.wait("tensor", "v", s_dve, st["dve"])
        pe(lambda e: e.matmul(plog[0:4, 0:1], oh164[:], prior[:], start=True, stop=True))
        sc.wait("vector", "p", s_pe, st["pe"])
        dve(lambda e: e.tensor_copy(out=pslots[:], in_=plog[0:4, 0:1]))
        sc.wait("vector", "v", s_dve, st["dve"])
        sc.wait("tensor", "v", s_dve, st["dve"])
        pe(lambda e: e.matmul(plog[0:4, 0:1], oh164[:], rb[:], start=True, stop=True))
        dve(lambda e: e.tensor_scalar_add(out=pslots[:], in0=pslots[:], scalar1=EPS_R))
        sc.wait("scalar", "v", s_dve, st["dve"])
        act(lambda e: e.activation(out=pslots[:], in_=pslots[:], func=AF.Ln))
        sc.wait("vector", "p", s_pe, st["pe"])
        sc.wait("vector", "a", s_act, st["act"])
        dve(lambda e: e.tensor_add(sbias[:], pslots[:], plog[0:4, 0:1]))
        sc.wait("vector", "v", s_dve, st["dve"])
        prolog_dve = st["dve"]

        ev_dve, ln2_act, tr_pe, sq_dve = {}, {}, {}, {}
        evac_act, yo_dve, out_at, wcast_dve = {}, {}, {}, {}

        for hf in range(2):
            base_tok = hf * HT
            for i in range(NTL):
                gi = hf * NTL + i
                row0 = base_tok + i * 128
                sc.wait("sync", "v", s_dve, ln2_act.get(gi - 1))
                sc.wait("sync", "v", s_dve, sq_dve.get(gi - 1))
                dma(xrow[:], xr_h[row0:row0 + 128, :])
                dma(yarow[:], yr_h[row0:row0 + 128, :])
                c0, c1, c2 = tc[:, 0:1], tc[:, 1:2], tc[:, 2:3]
                sc.wait("vector", "di", d_in, 16 * st["din"])
                dve(lambda e: e.tensor_add(x1[:], xrow[:], yarow[:]))
                sc.wait("vector", "v", s_dve, st["dve"])
                dve(lambda e, c=c0: e.reduce_sum(out=c, in_=x1[:], axis=AX.X))
                sc.wait("vector", "v", s_dve, st["dve"])
                dve(lambda e, c=c0: e.tensor_scalar_mul(out=c, in0=c, scalar1=1.0 / D))
                sc.wait("scalar", "v", s_dve, st["dve"])
                act(lambda e: e.activation(out=yarow[:], in_=x1[:], func=AF.Square))
                sc.wait("vector", "a", s_act, st["act"])
                dve(lambda e, c=c1: e.reduce_sum(out=c, in_=yarow[:], axis=AX.X))
                sq_dve[gi] = st["dve"]
                sc.wait("vector", "v", s_dve, st["dve"])
                dve(lambda e, c=c1: e.tensor_scalar_mul(out=c, in0=c, scalar1=1.0 / D))
                sc.wait("vector", "v", s_dve, st["dve"])
                dve(lambda e, a=c0, c=c2: e.tensor_mul(c, a, a))
                sc.wait("vector", "v", s_dve, st["dve"])
                dve(lambda e, a=c1, c=c2: e.tensor_sub(a, a, c))
                sc.wait("vector", "v", s_dve, st["dve"])
                dve(lambda e, c=c1: e.tensor_scalar_add(out=c, in0=c, scalar1=EPS_LN))
                sc.wait("scalar", "v", s_dve, st["dve"])
                act(lambda e, c=c1: e.activation(out=c, in_=c, func=AF.Ln))
                sc.wait("vector", "a", s_act, st["act"])
                dve(lambda e, c=c1: e.tensor_scalar_mul(out=c, in0=c, scalar1=-0.5))
                sc.wait("scalar", "v", s_dve, st["dve"])
                act(lambda e, c=c1: e.activation(out=c, in_=c, func=AF.Exp))
                sc.wait("vector", "a", s_act, st["act"])
                dve(lambda e, a=c0, b_=c1, c=c2: e.tensor_mul(c, a, b_))
                sc.wait("vector", "v", s_dve, st["dve"])
                dve(lambda e, c=c2: e.tensor_scalar_mul(out=c, in0=c, scalar1=-1.0))
                sc.wait("vector", "v", s_dve, st["dve"])
                sc.wait("vector", "p", s_pe, tr_pe.get(gi - 1))
                dve(lambda e, r=c1, m=c2: e.tensor_scalar(out=xt1b[:], in0=x1[:], scalar1=r, scalar2=m, op0=ALU.mult, op1=ALU.add))
                ln2_act[gi] = st["dve"]
                if debug and gi == 0:
                    dump("x1", x1[:])
                    dve(lambda e: e.tensor_copy(out=wblk[:].rearrange("p a b -> p (a b)"), in_=xt1b[:]))
                    sc.wait("vector", "v", s_dve, st["dve"])
                    dump("xt1", wblk[:].rearrange("p a b -> p (a b)"))
                for j in range(KD):
                    pj = j % 2
                    sc.wait("tensor", "v", s_dve, ln2_act[gi])
                    prev = ev_dve.get((gi, j - 2)) if j >= 2 else ev_dve.get((gi - 1, 2 * KD - 2 + j))
                    sc.wait("tensor", "v", s_dve, prev)
                    pe(lambda e, j_=j, pj_=pj: e.transpose(ptrb[pj_][:], xt1b[:, j_ * 128:(j_ + 1) * 128], identb[:]))
                    sc.wait("vector", "p", s_pe, st["pe"])
                    dve(lambda e, j_=j, pj_=pj, i_=i: e.tensor_copy(out=xT[:, j_, i_ * 128:(i_ + 1) * 128], in_=ptrb[pj_][:]))
                    ev_dve[(gi, j)] = st["dve"]
                for j in range(KD):
                    pj = j % 2
                    sc.wait("tensor", "v", s_dve, ev_dve.get((gi, KD + j - 2)) if j >= 2 else None)
                    pe(lambda e, j_=j, pj_=pj: e.transpose(ptrf[pj_][:], x1[:, j_ * 128:(j_ + 1) * 128], identf[:]))
                    sc.wait("vector", "p", s_pe, st["pe"])
                    sc.wait("vector", "x1", s_x1, 16 * (n_x1[0] - 1) if n_x1[0] >= 1 else None)
                    dve(lambda e, j_=j, pj_=pj: e.tensor_copy(out=x1stg[:, j_ % 2, :], in_=ptrf[pj_][:]))
                    ev_dve[(gi, KD + j)] = st["dve"]
                    sc.wait("sync", "v", s_dve, st["dve"])
                    dma_x1(x1sc[j * 128:(j + 1) * 128, row0:row0 + 128], x1stg[:, j % 2, :])
                tr_pe[gi] = st["pe"]

            ph1_dve = st["dve"]
            if phases < 1.1:
                continue

            for lc2 in range(HT // 512):
                sc.wait("tensor", "v", s_dve, max(ph1_dve, st["dve"]))
                for kc in range(KD):
                    pe(lambda e, k=kc, l2=lc2: e.matmul(pz[:], lATb[:, k, :], xT[:, k, l2 * 512:(l2 + 1) * 512], start=(k == 0), stop=(k == KD - 1)))
                sc.wait("vector", "p", s_pe, st["pe"])
                dve(lambda e, l2=lc2: e.tensor_copy(out=zall[:, l2 * 512:(l2 + 1) * 512], in_=pz[:]))
            if phases < 1.4:
                continue
            sc.wait("tensor", "v", s_dve, prolog_dve)
            for lc2 in range(HT // 512):
                sc.wait("tensor", "v", s_dve, st["dve"])
                for kc in range(KD):
                    pe(lambda e, k=kc, l2=lc2: e.matmul(plog[:], rwTb[:, k, :], xT[:, k, l2 * 512:(l2 + 1) * 512], start=(k == 0), stop=(k == KD - 1)))
                sc.wait("vector", "p", s_pe, st["pe"])
                dve(lambda e, l2=lc2: e.tensor_scalar_add(out=logits[:, l2 * 512:(l2 + 1) * 512], in0=plog[:], scalar1=rb[:]))
            if phases < 1.6:
                continue
            sc.wait("tensor", "v", s_dve, st["dve"])
            for lc2 in range(HT // 512):
                sc.wait("tensor", "v", s_dve, st["dve"])
                pe(lambda e, l2=lc2: e.matmul(plog[0:4, :], oh164[:], logits[:, l2 * 512:(l2 + 1) * 512], start=True, stop=True))
                sc.wait("vector", "p", s_pe, st["pe"])
                dve(lambda e, l2=lc2: e.tensor_scalar_add(out=lslot[:, l2 * 512:(l2 + 1) * 512], in0=plog[0:4, :], scalar1=sbias[:]))
            sc.wait("tensor", "v", s_dve, st["dve"])
            for i in range(NTL):
                pj = i % 2
                sc.wait("tensor", "v", s_dve, ev_dve.get(("lt", hf, i - 2)) if i >= 2 else st["dve"])
                pe(lambda e, i_=i, pj_=pj: e.transpose(ptrf[pj_][0:128, 0:4], lslot[:, i_ * 128:(i_ + 1) * 128], identf[0:4, 0:4]))
                sc.wait("vector", "p", s_pe, st["pe"])
                dve(lambda e, i_=i, pj_=pj: e.tensor_copy(out=ltok[:, i_, :], in_=ptrf[pj_][0:128, 0:4]))
                ev_dve[("lt", hf, i)] = st["dve"]
            if phases < 1.8:
                continue
            c0, c1 = tc[:, 0:1], tc[:, 1:2]
            for i in range(NTL):
                dve(lambda e, i_=i, c=c0: e.reduce_max(out=c, in_=ltok[:, i_, :], axis=AX.X))
                sc.wait("vector", "v", s_dve, st["dve"])
                dve(lambda e, c=c0: e.tensor_scalar_mul(out=c, in0=c, scalar1=-1.0))
                sc.wait("scalar", "v", s_dve, st["dve"])
                act(lambda e, i_=i, c=c0: e.activation(out=atok[:, i_, :], in_=ltok[:, i_, :], func=AF.Exp, bias=c))
                sc.wait("vector", "a", s_act, st["act"])
                dve(lambda e, i_=i, c=c1: e.reduce_sum(out=c, in_=atok[:, i_, :], axis=AX.X))
                sc.wait("vector", "v", s_dve, st["dve"])
                dve(lambda e, c=c1: e.reciprocal(out=c, in_=c))
                sc.wait("vector", "v", s_dve, st["dve"])
                dve(lambda e, i_=i, c=c1: e.tensor_scalar(out=atok[:, i_, :], in0=atok[:, i_, :], scalar1=c, scalar2=None, op0=ALU.mult))
            for i in range(NTL):
                pj = i % 2
                sc.wait("tensor", "v", s_dve, ev_dve.get(("at", hf, i - 2)) if i >= 2 else st["dve"])
                pe(lambda e, i_=i, pj_=pj: e.transpose(ptrf[pj_][0:4, 0:128], atok[:, i_, :], identf[:]))
                sc.wait("vector", "p", s_pe, st["pe"])
                dve(lambda e, i_=i, pj_=pj: e.tensor_copy(out=aslT[:, i_ * 128:(i_ + 1) * 128], in_=ptrf[pj_][0:4, 0:128]))
                ev_dve[("at", hf, i)] = st["dve"]
            if phases < 1.9:
                continue
            sc.wait("tensor", "v", s_dve, st["dve"])
            for lc2 in range(HT // 512):
                sc.wait("tensor", "v", s_dve, st["dve"])
                pe(lambda e, l2=lc2: e.matmul(plog[:, :], oh46[:], aslT[:, l2 * 512:(l2 + 1) * 512], start=True, stop=True))
                sc.wait("vector", "p", s_pe, st["pe"])
                dve(lambda e, l2=lc2: e.tensor_copy(out=gateT[:, l2 * 512:(l2 + 1) * 512], in_=plog[:]))
            sc.wait("sync", "v", s_dve, st["dve"])
            dma(gate_sc, gateT[:])
            sc.wait("sync", "di", d_in, 16 * st["din"])
            for e16 in range(16):
                dma(gatebc[e16 * 8:(e16 + 1) * 8, :], gate_sc[e16:e16 + 1, :].partition_broadcast(8))
            sc.wait("vector", "di", d_in, 16 * st["din"])
            if phases < 1.95:
                continue
            if debug:
                dump("zall", zall[:])
                dump("logits", logits[:])
                dump("lslot", lslot[:])
                dump("atok", atok[:])
                dump("gateT", gateT[:])
                dump("gbc", gatebc[:])
            dve(lambda e: e.scalar_tensor_tensor(out=wlorab[:], in0=zall[:], scalar=SCALING, in1=gatebc[:], op0=ALU.mult, op1=ALU.mult))
            lora_dve = st["dve"]
            if debug:
                dve(lambda e: e.tensor_copy(out=zall[:], in_=wlorab[:]))
                sc.wait("vector", "v", s_dve, st["dve"])
                dump("wl", zall[:])
                dve(lambda e: e.scalar_tensor_tensor(out=zall[:], in0=wlorab[:], scalar=1.0, in1=wlorab[:], op0=ALU.mult, op1=ALU.bypass))
            if phases < 3:
                continue

            for ot in range(32):
                po = ot % 2
                gidx0 = hf * 64 + ot * 2
                sc.wait("scalar", "v", s_dve, wcast_dve.get(hf * 32 + ot - 1))
                dma_w(wblk[:], wt_h[ot])
                sc.wait("vector", "dw", d_w, 16 * n_dw[0])
                sc.wait("vector", "p", s_pe, tr_pe.get(hf * NTL + NTL - 1) if ot < 2 else out_at.get(("pe", gidx0 - 3)))
                dve(lambda e, p=po: e.tensor_copy(out=wblkb[:, p, :, :], in_=wblk[:]))
                wcast_dve[hf * 32 + ot] = st["dve"]
                for lc2 in range(2):
                    gidx = gidx0 + lc2
                    pp = gidx % 2
                    sc.wait("tensor", "v", s_dve, max(wcast_dve[hf * 32 + ot], lora_dve, yo_dve.get(gidx - 2) or 0))
                    for kc in range(KD):
                        pe(lambda e, k=kc, p=po, l2=lc2, q=pp: e.matmul(pw[q][:], wblkb[:, p, k, :], xT[:, k, l2 * 512:(l2 + 1) * 512], start=(k == 0), stop=False))
                    pe(lambda e, l2=lc2, q=pp, o=ot: e.matmul(pw[q][:], lBTb[:, o * 128:(o + 1) * 128], wlorab[:, l2 * 512:(l2 + 1) * 512], start=False, stop=True))
                    out_at[("pe", gidx)] = st["pe"]
                    sc.wait("sync", "x1", s_x1, 16 * n_x1[0])
                    sc.wait("sync", "v", s_dve, yo_dve.get(gidx - 2))
                    dma(x1t[:, pp, :], x1sc[ot * 128:(ot + 1) * 128, base_tok + lc2 * 512: base_tok + (lc2 + 1) * 512])
                    sc.wait("vector", "di", d_in, 16 * st["din"])
                    sc.wait("vector", "p", s_pe, st["pe"])
                    oc = out_at.get(("out", gidx - 2))
                    sc.wait("vector", "o", s_out, 16 * oc if oc else None)
                    dve(lambda e, q=pp, o=ot: e.scalar_tensor_tensor(out=yo[:, q, :], in0=pw[q][:], scalar=bbs[:, o:o + 1], in1=x1t[:, q, :], op0=ALU.add, op1=ALU.add))
                    yo_dve[gidx] = st["dve"]
                    sc.wait("sync", "v", s_dve, yo_dve[gidx])
                    dma_out(yo_h[ot * 128:(ot + 1) * 128, base_tok + lc2 * 512: base_tok + (lc2 + 1) * 512], yo[:, pp, :])
                    out_at[("out", gidx)] = n_out[0]

            if hf == 0:
                sc.wait("vector", "p", s_pe, st["pe"])
                sc.wait("scalar", "p", s_pe, st["pe"])

        if phases >= 3:
            sc.emit("sync", lambda e: e.wait_ge(s_out, 16 * n_out[0]))
        else:
            sc.wait("sync", "x1f", s_x1, 16 * n_x1[0])
            sc.wait("sync", "v", s_dve, st["dve"])
            sc.wait("sync", "a", s_act, st["act"])
            sc.wait("sync", "p", s_pe, st["pe"])

        with nc.Block() as block:
            @block.sync
            def _(eng):
                sc.replay("sync", eng)

            @block.vector
            def _(eng):
                sc.replay("vector", eng)

            @block.scalar
            def _(eng):
                sc.replay("scalar", eng)

            @block.tensor
            def _(eng):
                sc.replay("tensor", eng)
    return nc


_CACHE = {}


def _programs():
    if "l0" not in _CACHE:
        _CACHE["l0"] = build_l0()
        _CACHE["l1"] = build_l1()
        _CACHE["l2"] = build_l2()
    return _CACHE["l0"], _CACHE["l1"], _CACHE["l2"]


def kernel(L0_attn_x, L0_attn_A_log, L0_attn_D, L0_attn_x_proj_w, L0_attn_dt_proj_w,
           L0_attn_dt_proj_b, L0_ffn_W_base, L0_ffn_b_base, L0_ffn_lora_A, L0_ffn_lora_B,
           L0_ffn_router_W, L0_ffn_router_b, L0_ffn_domain_prior, L0_ffn_active):
    x = np.ascontiguousarray(np.asarray(L0_attn_x, dtype=np.float32))
    x2d = x.reshape(B * L, D)
    f32 = lambda a: np.ascontiguousarray(np.asarray(a, dtype=np.float32))
    identf = np.eye(128, dtype=np.float32)

    nc0, nc1, nc2 = _programs()

    xpwT = np.ascontiguousarray(f32(L0_attn_x_proj_w).T)
    in0 = [dict(xrows=x2d[k * TOK:(k + 1) * TOK], xpwT=xpwT, identf=identf) for k in range(NCORES)]
    r0 = run_bass_kernel_spmd(nc0, in0, core_ids=list(range(NCORES))).results
    stats_all = np.concatenate(
        [np.asarray(r0[k]["stats"]).transpose(1, 0, 2).reshape(TOK, 2) for k in range(NCORES)], axis=0)
    xp_all = np.concatenate(
        [np.concatenate([r0[k]["xp_hi"], r0[k]["xp_lo"]], axis=0) for k in range(NCORES)], axis=1)

    dtwT = np.ascontiguousarray(f32(L0_attn_dt_proj_w).T)
    dtb = f32(L0_attn_dt_proj_b)
    alog = f32(L0_attn_A_log)
    dvec = f32(L0_attn_D)
    in1 = []
    for core in range(NCORES):
        b, q = core // 4, core % 4
        qs = slice(q * DQ, (q + 1) * DQ)
        in1.append(dict(
            xqT=np.ascontiguousarray(x[b].T[qs]),
            statsT=np.ascontiguousarray(stats_all[b * L:(b + 1) * L].T),
            xpT=np.ascontiguousarray(xp_all[:, b * L:(b + 1) * L]),
            dtwT=np.ascontiguousarray(dtwT[:, qs]),
            dtb=np.ascontiguousarray(dtb[qs].reshape(NT, 128).T),
            alog=np.ascontiguousarray(alog[qs].reshape(NT, 128, NS).transpose(1, 0, 2)),
            dvec=np.ascontiguousarray(dvec[qs].reshape(NT, 128).T),
        ))
    r1 = run_bass_kernel_spmd(nc1, in1, core_ids=list(range(NCORES))).results
    ya = np.empty((B * L, D), dtype=np.float32)
    for b in range(B):
        yab = np.concatenate([r1[b * 4 + q]["ya"] for q in range(4)], axis=0)
        ya[b * L:(b + 1) * L] = yab.T

    WTb = np.ascontiguousarray(
        f32(L0_ffn_W_base).T.reshape(KD, 128, 32, 128).transpose(2, 1, 0, 3))
    bb = np.ascontiguousarray(f32(L0_ffn_b_base).reshape(32, 128).T)
    loraAT = np.ascontiguousarray(f32(L0_ffn_lora_A).reshape(128, D).T)
    loraBT = np.ascontiguousarray(f32(L0_ffn_lora_B).transpose(0, 2, 1).reshape(128, D))
    rwT = np.ascontiguousarray(f32(L0_ffn_router_W).T)
    rb = f32(L0_ffn_router_b).reshape(16, 1)
    prior = f32(L0_ffn_domain_prior).reshape(16, 1)
    act4 = np.ascontiguousarray(np.asarray(L0_ffn_active, dtype=np.int32).reshape(4, 1))
    actrow = np.ascontiguousarray(np.asarray(L0_ffn_active, dtype=np.int32).reshape(1, 4))
    io16 = np.arange(16, dtype=np.float32).reshape(16, 1)
    iotar = np.arange(16, dtype=np.float32).reshape(1, 16)
    in2 = []
    for k in range(NCORES):
        rs = slice(k * TOK, (k + 1) * TOK)
        in2.append(dict(
            xrows=x2d[rs], yarows=ya[rs], WTb=WTb, bb=bb, loraAT=loraAT, loraBT=loraBT,
            rwT=rwT, rb=rb, prior=prior, act4=act4, actrow=actrow, iota16=io16, iotar=iotar,
            identf=identf,
        ))
    r2 = run_bass_kernel_spmd(nc2, in2, core_ids=list(range(NCORES))).results
    out = np.empty((B * L, D), dtype=np.float32)
    for k in range(NCORES):
        out[k * TOK:(k + 1) * TOK] = r2[k]["youtT"].T
    return out.reshape(B, L, D)


# version 04
# speedup vs baseline: 1.0358x; 1.0358x over previous
import numpy as np
import ml_dtypes
import contextlib
import concourse.bass as bass
import concourse.mybir as mybir
from concourse.bass_utils import run_bass_kernel_spmd

F32 = mybir.dt.float32
BF16 = mybir.dt.bfloat16
I32 = mybir.dt.int32
AF = mybir.ActivationFunctionType
ALU = mybir.AluOpType
AX = mybir.AxisListType

B, L, D = 2, 8192, 4096
NCORES = 8
NS = 16
EPS_LN = 1e-5
EPS_R = 1e-12
SCALING = 2.0
TOK = (B * L) // NCORES
DQ = D // 4
LC = 512
NCH = L // LC
NT = DQ // 128
KD = D // 128


class Sched:
    def __init__(self, engines):
        self.ops = {e: [] for e in engines}
        self.guar = {e: {} for e in engines}

    def emit(self, eng, fn):
        self.ops[eng].append(fn)

    def wait(self, eng, key, sem, val):
        if val is None or val <= 0:
            return
        if val <= self.guar[eng].get(key, 0):
            return
        self.guar[eng][key] = val
        self.ops[eng].append(lambda e, s=sem, v=val: e.wait_ge(s, v))

    def replay(self, eng, handle):
        for fn in self.ops[eng]:
            fn(handle)


def _mk_emitters(sc, sems):
    state = {"act": 0, "dve": 0, "pe": 0, "din": 0}

    def dma(out, in_, sp=False):
        state["din"] += 1
        sc.emit("sync", lambda e, o=out, i=in_: e.dma_start(out=o, in_=i).then_inc(sems["d_in"], 16))

    def act(fn):
        state["act"] += 1
        sc.emit("scalar", lambda e, f=fn: f(e).then_inc(sems["s_act"], 1))

    def dve(fn):
        state["dve"] += 1
        sc.emit("vector", lambda e, f=fn: f(e).then_inc(sems["s_dve"], 1))

    def pe(fn):
        state["pe"] += 1
        sc.emit("tensor", lambda e, f=fn: f(e).then_inc(sems["s_pe"], 1))

    return dma, act, dve, pe, state



def build_l0():
    nc = bass.Bass("TRN2", target_bir_lowering=False)
    xr_h = nc.dram_tensor("xrows", [TOK, D], F32, kind="ExternalInput").ap()
    wp_h = nc.dram_tensor("xpwT", [D, 160], F32, kind="ExternalInput").ap()
    idb_h = nc.dram_tensor("identf", [128, 128], F32, kind="ExternalInput").ap()
    st_h = nc.dram_tensor("stats", [128, TOK // 128, 2], F32, kind="ExternalOutput").ap()
    hi_h = nc.dram_tensor("xp_hi", [128, TOK], F32, kind="ExternalOutput").ap()
    lo_h = nc.dram_tensor("xp_lo", [32, TOK], F32, kind="ExternalOutput").ap()
    NTL = TOK // 128

    with contextlib.ExitStack() as ctx:
        sbt = lambda n, s, d: ctx.enter_context(nc.sbuf_tensor(n, s, d))
        pst = lambda n, s, d: ctx.enter_context(nc.psum_tensor(n, s, d))
        sem = lambda n: ctx.enter_context(nc.semaphore(n))

        xrow = sbt("xrow", [128, 2, D], F32)
        sq = sbt("sq", [128, D], F32)
        xlnb = sbt("xlnb", [128, 2, D], BF16)
        xlnT = sbt("xlnT", [128, KD, 128], BF16)
        wp = sbt("wp", [128, KD, 160], F32)
        wpb = sbt("wpb", [128, KD, 160], BF16)
        identb = sbt("ident", [128, 128], BF16)
        identf_stg = sbt("identf_stg", [128, 128], F32)
        stats = sbt("stats_sb", [128, NTL, 2], F32)
        tc = sbt("tc", [128, 4], F32)
        xph = sbt("xph", [128, NTL, 128], F32)
        xpl = sbt("xpl", [32, NTL, 128], F32)
        ptr = [pst(f"ptr{i}", [128, 128], BF16) for i in range(2)]
        pxh = pst("pxh", [128, 128], F32)
        pxl = pst("pxl", [32, 128], F32)

        sems = dict(d_in=sem("d_in"), s_act=sem("s_act"), s_dve=sem("s_dve"), s_pe=sem("s_pe"))
        s_out = sem("s_out")
        sc = Sched(["sync", "vector", "scalar", "tensor"])
        dma, act, dve, pe, st = _mk_emitters(sc, sems)
        d_in, s_act, s_dve, s_pe = sems["d_in"], sems["s_act"], sems["s_dve"], sems["s_pe"]

        for kc in range(KD):
            dma(wp[:, kc, :], wp_h.rearrange("(c p) m -> p c m", p=128)[:, kc, :])
        dma(identf_stg[:], idb_h)
        sc.wait("vector", "di", d_in, 16 * st["din"])
        dve(lambda e: e.tensor_copy(out=wpb[:], in_=wp[:]))
        dve(lambda e: e.tensor_copy(out=identb[:], in_=identf_stg[:]))

        ln_act, evac_dve, trans_pe, mmev_dve = {}, {}, {}, {}

        for i in range(NTL):
            par = i % 2
            if i >= 2:
                sc.wait("sync", "v", s_dve, ln_act[i - 2])
            dma(xrow[:, par, :], xr_h.rearrange("(t p) d -> t p d", p=128)[i])
            c0, c1, c2 = tc[:, 0:1], tc[:, 1:2], tc[:, 2:3]
            mu, rstd = stats[:, i, 0:1], stats[:, i, 1:2]

            sc.wait("vector", "di", d_in, 16 * st["din"])
            dve(lambda e, p=par, c=c0: e.reduce_sum(out=c, in_=xrow[:, p, :], axis=AX.X))
            sc.wait("vector", "v", s_dve, st["dve"])
            dve(lambda e, c=c0, m=mu: e.tensor_scalar_mul(out=m, in0=c, scalar1=1.0 / D))
            sc.wait("scalar", "di", d_in, 16 * st["din"])
            act(lambda e, p=par: e.activation(out=sq[:], in_=xrow[:, p, :], func=AF.Square))
            sc.wait("vector", "a", s_act, st["act"])
            dve(lambda e, c=c1: e.reduce_sum(out=c, in_=sq[:], axis=AX.X))
            sc.wait("vector", "v", s_dve, st["dve"])
            dve(lambda e, c=c1: e.tensor_scalar_mul(out=c, in0=c, scalar1=1.0 / D))
            sc.wait("vector", "v", s_dve, st["dve"])
            dve(lambda e, m=mu, c=c2: e.tensor_mul(c, m, m))
            sc.wait("vector", "v", s_dve, st["dve"])
            dve(lambda e, a=c1, b_=c2: e.tensor_sub(a, a, b_))
            sc.wait("vector", "v", s_dve, st["dve"])
            dve(lambda e, c=c1: e.tensor_scalar_add(out=c, in0=c, scalar1=EPS_LN))
            sc.wait("scalar", "v", s_dve, st["dve"])
            act(lambda e, c=c1: e.activation(out=c, in_=c, func=AF.Ln))
            sc.wait("vector", "a", s_act, st["act"])
            dve(lambda e, c=c1: e.tensor_scalar_mul(out=c, in0=c, scalar1=-0.5))
            sc.wait("scalar", "v", s_dve, st["dve"])
            act(lambda e, c=c1, r=rstd: e.activation(out=r, in_=c, func=AF.Exp))
            sc.wait("vector", "a", s_act, st["act"])
            dve(lambda e, m=mu, r=rstd, c=c2: e.tensor_mul(c, m, r))
            sc.wait("vector", "v", s_dve, st["dve"])
            dve(lambda e, c=c2: e.tensor_scalar_mul(out=c, in0=c, scalar1=-1.0))
            sc.wait("vector", "v", s_dve, st["dve"])
            if i >= 2:
                sc.wait("vector", "p", s_pe, trans_pe[i - 2])
            dve(lambda e, p=par, r=rstd, c=c2: e.tensor_scalar(out=xlnb[:, p, :], in0=xrow[:, p, :], scalar1=r, scalar2=c, op0=ALU.mult, op1=ALU.add))
            ln_act[i] = st["dve"]

            for j in range(KD):
                pj = j % 2
                sc.wait("tensor", "v", s_dve, ln_act[i])
                prev = evac_dve.get((i, j - 2)) if j >= 2 else (evac_dve.get((i - 1, KD - 2 + j)) if i >= 1 else None)
                sc.wait("tensor", "v", s_dve, prev)
                pe(lambda e, p=par, j_=j, pj_=pj: e.transpose(ptr[pj_][:], xlnb[:, p, j_ * 128:(j_ + 1) * 128], identb[:]))
                sc.wait("vector", "p", s_pe, st["pe"])
                dve(lambda e, j_=j, pj_=pj: e.tensor_copy(out=xlnT[:, j_, :], in_=ptr[pj_][:]))
                evac_dve[(i, j)] = st["dve"]
            trans_pe[i] = st["pe"]

            sc.wait("tensor", "v", s_dve, evac_dve[(i, KD - 1)])
            if i >= 1:
                sc.wait("tensor", "v", s_dve, mmev_dve[i - 1])
            for kc in range(KD):
                pe(lambda e, k=kc: e.matmul(pxh[:], wpb[:, k, 0:128], xlnT[:, k, :], start=(k == 0), stop=(k == KD - 1)))
            for kc in range(KD):
                pe(lambda e, k=kc: e.matmul(pxl[:], wpb[:, k, 128:160], xlnT[:, k, :], start=(k == 0), stop=(k == KD - 1)))
            sc.wait("vector", "p", s_pe, st["pe"])
            dve(lambda e, i_=i: e.tensor_copy(out=xph[:, i_, :], in_=pxh[:]))
            dve(lambda e, i_=i: e.tensor_copy(out=xpl[:, i_, :], in_=pxl[:, :]))
            mmev_dve[i] = st["dve"]

        sc.wait("sync", "v", s_dve, st["dve"])
        sc.wait("sync", "a", s_act, st["act"])
        sc.emit("sync", lambda e: e.dma_start(out=st_h, in_=stats[:]).then_inc(s_out, 16))
        sc.emit("sync", lambda e: e.dma_start(out=hi_h.rearrange("p (t l) -> p t l", l=128), in_=xph[:]).then_inc(s_out, 16))
        sc.emit("sync", lambda e: e.dma_start(out=lo_h.rearrange("p (t l) -> p t l", l=128), in_=xpl[:]).then_inc(s_out, 16))
        sc.emit("sync", lambda e: e.wait_ge(s_out, 48))

        with nc.Block() as block:
            @block.sync
            def _(eng):
                sc.replay("sync", eng)

            @block.vector
            def _(eng):
                sc.replay("vector", eng)

            @block.scalar
            def _(eng):
                sc.replay("scalar", eng)

            @block.tensor
            def _(eng):
                sc.replay("tensor", eng)
    return nc



def build_l1(nch=NCH, debug=False):
    nc = bass.Bass("TRN2", target_bir_lowering=False)
    xqT_h = nc.dram_tensor("xqT", [DQ, L], F32, kind="ExternalInput").ap()
    stT_h = nc.dram_tensor("statsT", [2, L], F32, kind="ExternalInput").ap()
    xpT_h = nc.dram_tensor("xpT", [160, L], F32, kind="ExternalInput").ap()
    dtw_h = nc.dram_tensor("dtwT", [128, DQ], F32, kind="ExternalInput").ap()
    dtb_h = nc.dram_tensor("dtb", [128, NT], F32, kind="ExternalInput").ap()
    alog_h = nc.dram_tensor("alog", [128, NT, NS], F32, kind="ExternalInput").ap()
    dvec_h = nc.dram_tensor("dvec", [128, NT], F32, kind="ExternalInput").ap()
    ya_h = nc.dram_tensor("ya", [DQ, L], F32, kind="ExternalOutput").ap()
    if debug == 2:
        dm2 = {nm: nc.dram_tensor("d2_" + nm, [128, 4, 128], F32, kind="ExternalOutput").ap()
               for nm in ["dA", "ub", "hb"]}
        for nm in ["dt", "xl", "mu", "rs", "xq"]:
            dm2[nm] = nc.dram_tensor("d2_" + nm, [128, 128], F32, kind="ExternalOutput").ap()
    if debug:
        dmp = {nm: nc.dram_tensor("dmp_" + nm, shp, F32, kind="ExternalOutput").ap()
               for nm, shp in [("xln", [128, NT, LC]), ("dtT", [128, NT, LC]), ("dtx", [128, NT, LC]),
                               ("Bbc", [128, 4, LC]), ("Cbc", [128, 4, LC]), ("dA", [128, 4, LC]),
                               ("ub", [128, 4, LC]), ("hb", [128, 4, LC])]}

    with contextlib.ExitStack() as ctx:
        sbt = lambda n, s, d: ctx.enter_context(nc.sbuf_tensor(n, s, d))
        pst = lambda n, s, d: ctx.enter_context(nc.psum_tensor(n, s, d))
        sem = lambda n: ctx.enter_context(nc.semaphore(n))

        xq = sbt("xq", [128, NT, LC], F32)
        xln = sbt("xln", [128, NT, LC], F32)
        dtT = sbt("dtT", [128, NT, LC], F32)
        dtx = sbt("dtx", [128, NT, LC], F32)
        Bbc = sbt("Bbc", [128, 2, 4, LC], F32)
        Cbc = sbt("Cbc", [128, 2, 4, LC], F32)
        dA = sbt("dA", [128, 4, 4, LC], F32)
        ub = sbt("ub", [128, 4, LC], F32)
        hb = sbt("hb", [128, 4, LC], F32)
        yb = sbt("yb", [128, NT, LC], F32)
        mubc = sbt("mubc", [128, 2, LC], F32)
        rsbc = sbt("rsbc", [128, 2, LC], F32)
        dtin = sbt("dtin", [128, 2, LC], F32)
        dtinb = sbt("dtinb", [128, 2, LC], BF16)
        esb = sbt("esb", [128, LC], F32)
        dtw = sbt("dtw_sb", [128, DQ], F32)
        dtwb = sbt("dtwb", [128, DQ], BF16)
        alog = sbt("alog_sb", [128, NT, NS], F32)
        Asb = sbt("Asb", [128, NT, NS], F32)
        dtb = sbt("dtb_sb", [128, NT], F32)
        dvec = sbt("dvec_sb", [128, NT], F32)
        carry = sbt("carry", [128, NT, NS], F32)
        pdt = [pst(f"pdt{i}", [128, LC], F32) for i in range(2)]
        if debug == 2:
            db2 = {nm: sbt("db2_" + nm, [128, 4, 128], F32) for nm in ["dA", "ub", "hb"]}
            for nm in ["dt", "xl", "mu", "rs", "xq"]:
                db2[nm] = sbt("db2_" + nm, [128, 128], F32)

        sems = dict(d_in=sem("d_in"), s_act=sem("s_act"), s_dve=sem("s_dve"), s_pe=sem("s_pe"))
        s_out = sem("s_out")
        d_rdy = sem("d_rdy")
        sc = Sched(["sync", "vector", "scalar", "tensor"])
        dma, act, dve, pe, st = _mk_emitters(sc, sems)
        d_in, s_act, s_dve, s_pe = sems["d_in"], sems["s_act"], sems["s_dve"], sems["s_pe"]
        n_out = [0]
        n_rdy = [0]

        def dbarrier():
            return st["din"]

        def dma_out(out, in_):
            n_out[0] += 1
            sc.emit("sync", lambda e, o=out, i=in_: e.dma_start(out=o, in_=i).then_inc(s_out, 16))

        def dump(name, src_ap):
            if not debug:
                return
            sc.wait("sync", "v", s_dve, st["dve"])
            sc.wait("sync", "a", s_act, st["act"])
            dma_out(dmp[name], src_ap)
            sc.wait("vector", "o", s_out, 16 * n_out[0])
            sc.wait("scalar", "o", s_out, 16 * n_out[0])

        dma(dtw[:], dtw_h)
        dma(dtb[:], dtb_h)
        dma(alog[:], alog_h)
        dma(dvec[:], dvec_h)
        rdy0 = dbarrier()
        sc.wait("vector", "di", d_in, 16 * rdy0)
        dve(lambda e: e.tensor_copy(out=dtwb[:], in_=dtw[:]))
        dve(lambda e: e.memset(carry[:], 0.0))
        sc.wait("scalar", "di", d_in, 16 * rdy0)
        act(lambda e: e.activation(out=Asb[:], in_=alog[:], func=AF.Exp))
        sc.wait("vector", "a", s_act, st["act"])
        dve(lambda e: e.tensor_scalar_mul(out=Asb[:], in0=Asb[:], scalar1=-1.0))
        asb_done = st["dve"]

        ln_done, cast_done, dtx_done = {}, {}, {}
        bc_used, da_used, exp_act = {}, {}, {}
        mm_pe, ya_done, out_cnt = {}, {}, {}
        din_after = {}

        bcdin = {}

        def emit_bc(G):
            if G >= nch * 4:
                return
            cg, g4 = G // 4, G % 4
            sc.wait("sync", "v", s_dve, bc_used.get(G - 2))
            for k in range(4):
                n = g4 * 4 + k
                dma(Bbc[:, G % 2, k, :], xpT_h[128 + n:129 + n, cg * LC:(cg + 1) * LC].partition_broadcast(128))
                dma(Cbc[:, G % 2, k, :], xpT_h[144 + n:145 + n, cg * LC:(cg + 1) * LC].partition_broadcast(128))
            bcdin[G] = dbarrier()

        def emit_loads(c):
            par = c % 2
            sc.wait("sync", "v", s_dve, ln_done.get(c - 1))
            for tq in range(NT):
                dma(xq[:, tq, :], xqT_h.rearrange("(t p) l -> p t l", p=128)[:, tq, c * LC:(c + 1) * LC])
            sc.wait("sync", "v", s_dve, cast_done.get(c - 2))
            dma(dtin[:, par, :], xpT_h[0:128, c * LC:(c + 1) * LC])
            dma(mubc[:, par, :], stT_h[0:1, c * LC:(c + 1) * LC].partition_broadcast(128))
            dma(rsbc[:, par, :], stT_h[1:2, c * LC:(c + 1) * LC].partition_broadcast(128))
            din_after[c] = dbarrier()

        emit_loads(0)
        emit_bc(0)
        emit_bc(1)

        for c in range(nch):
            par = c % 2
            sc.wait("vector", "di", d_in, 16 * din_after[c])
            sc.wait("vector", "p", s_pe, mm_pe.get(c - 2))
            dve(lambda e, p=par: e.tensor_copy(out=dtinb[:, p, :], in_=dtin[:, p, :]))
            cast_done[c] = st["dve"]
            for t in range(NT):
                dve(lambda e, t_=t, p=par: e.tensor_sub(xln[:, t_, :], xq[:, t_, :], mubc[:, p, :]))
                sc.wait("vector", "v", s_dve, st["dve"])
                dve(lambda e, t_=t, p=par: e.tensor_mul(xln[:, t_, :], xln[:, t_, :], rsbc[:, p, :]))
                sc.wait("vector", "v", s_dve, st["dve"])
            if debug == 2 and c == 0:
                dve(lambda e: e.tensor_copy(out=db2["xq"][:], in_=xq[:, 0, 384:512]))
                dve(lambda e: e.tensor_copy(out=db2["mu"][:], in_=mubc[:, 0, 384:512]))
                dve(lambda e: e.tensor_copy(out=db2["rs"][:], in_=rsbc[:, 0, 384:512]))
                dve(lambda e: e.tensor_copy(out=db2["xl"][:], in_=xln[:, 0, 384:512]))
            ln_done[c] = st["dve"]
            if c == 0:
                dump("xln", xln[:])
            if c + 1 < nch:
                emit_loads(c + 1)

            for t in range(NT):
                sc.wait("tensor", "v", s_dve, cast_done[c])
                prev_exp = exp_act.get((c, t - 2)) if t >= 2 else exp_act.get((c - 1, NT - 2 + t))
                sc.wait("tensor", "a", s_act, prev_exp)
                pe(lambda e, t_=t, p=par: e.matmul(pdt[t_ % 2][:], dtwb[:, t_ * 128:(t_ + 1) * 128], dtinb[:, p, :], start=True, stop=True))
                sc.wait("scalar", "p", s_pe, st["pe"])
                sc.wait("scalar", "v", s_dve, dtx_done.get(c - 1))
                act(lambda e, t_=t: e.activation(out=esb[:], in_=pdt[t_ % 2][:], func=AF.Exp, bias=dtb[:, t_:t_ + 1]))
                exp_act[(c, t)] = st["act"]
                act(lambda e, t_=t: e.activation(out=dtT[:, t_, :], in_=esb[:], func=AF.Ln, bias=1.0))
            mm_pe[c] = st["pe"]
            sp_done = st["act"]
            if c == 0:
                dump("dtT", dtT[:])

            sc.wait("vector", "a", s_act, sp_done)
            for t in range(NT):
                dve(lambda e, t_=t: e.tensor_mul(dtx[:, t_, :], dtT[:, t_, :], xln[:, t_, :]))
            dtx_done[c] = st["dve"]
            if c == 0:
                dump("dtx", dtx[:])

            for g4 in range(4):
                G = c * 4 + g4
                for t in range(NT):
                    gidx = (G * NT + t) % 4
                    sc.wait("scalar", "v", s_dve, max(asb_done, da_used.get(gidx) or 0))
                    for k in range(4):
                        n = g4 * 4 + k
                        act(lambda e, t_=t, k_=k, n_=n, s=gidx: e.activation(
                            out=dA[:, s, k_, :], in_=dtT[:, t_, :], func=AF.Exp, scale=Asb[:, t_, n_:n_ + 1]))
                    da_act = st["act"]
                    sc.wait("vector", "di", d_in, 16 * bcdin[G])
                    sc.wait("vector", "a", s_act, da_act)
                    if debug and G == 0 and t == 0:
                        dump("Bbc", Bbc[:, 0, :, :])
                        dump("Cbc", Cbc[:, 0, :, :])
                        dump("dA", dA[:, 0, :, :])
                    dve(lambda e, t_=t, g=G: e.tensor_mul(
                        ub[:], dtx[:, t_, :][:, None, :].broadcast_to([128, 4, LC]), Bbc[:, g % 2, :, :]))
                    if debug and G == 0 and t == 0:
                        dump("ub", ub[:])
                    for k in range(4):
                        n = g4 * 4 + k
                        dve(lambda e, t_=t, k_=k, n_=n, s=gidx: e.tensor_tensor_scan(
                            out=hb[:, k_, :], data0=dA[:, s, k_, :], data1=ub[:, k_, :],
                            initial=carry[:, t_, n_:n_ + 1], op0=ALU.mult, op1=ALU.add))
                    if debug == 2 and G == 0 and t == 0:
                        dve(lambda e, s=gidx: e.tensor_copy(out=db2["dA"][:], in_=dA[:, s, :, 384:512]))
                        dve(lambda e: e.tensor_copy(out=db2["ub"][:], in_=ub[:, :, 384:512]))
                        dve(lambda e: e.tensor_copy(out=db2["hb"][:], in_=hb[:, :, 384:512]))
                        dve(lambda e: e.tensor_copy(out=db2["dt"][:], in_=dtT[:, 0, 384:512]))

                    da_used[gidx] = st["dve"]
                    if debug and G == 0 and t == 0:
                        dump("hb", hb[:])
                    sc.wait("vector", "v", s_dve, st["dve"])
                    dve(lambda e, t_=t, g4_=g4: e.tensor_copy(
                        out=carry[:, t_, g4_ * 4:(g4_ + 1) * 4], in_=hb[:, :, LC - 1]))
                    sc.wait("vector", "v", s_dve, st["dve"])
                    dve(lambda e, g=G: e.tensor_mul(hb[:], hb[:], Cbc[:, g % 2, :, :]))
                    dve(lambda e: e.tensor_add(hb[:, 0:2, :], hb[:, 0:2, :], hb[:, 2:4, :]))
                    if g4 == 0:
                        oc = out_cnt.get((c - 1, t))
                        sc.wait("vector", "o", s_out, 16 * oc if oc else None)
                        dve(lambda e, t_=t: e.tensor_add(yb[:, t_, :], hb[:, 0, :], hb[:, 1, :]))
                    else:
                        dve(lambda e: e.tensor_add(hb[:, 0, :], hb[:, 0, :], hb[:, 1, :]))
                        dve(lambda e, t_=t: e.tensor_add(yb[:, t_, :], yb[:, t_, :], hb[:, 0, :]))
                bc_used[G] = st["dve"]
                emit_bc(G + 2)

            for t in range(NT):
                dve(lambda e, t_=t: e.scalar_tensor_tensor(
                    out=yb[:, t_, :], in0=xln[:, t_, :], scalar=dvec[:, t_:t_ + 1], in1=yb[:, t_, :],
                    op0=ALU.mult, op1=ALU.add))
                ya_done[(c, t)] = st["dve"]
                sc.wait("sync", "v", s_dve, ya_done[(c, t)])
                dma_out(ya_h[t * 128:(t + 1) * 128, c * LC:(c + 1) * LC], yb[:, t, :])
                out_cnt[(c, t)] = n_out[0]

        if debug == 2:
            sc.wait("sync", "v", s_dve, st["dve"])
            for nm in ["dA", "ub", "hb", "dt", "xl", "mu", "rs", "xq"]:
                dma_out(dm2[nm], db2[nm][:])
        sc.emit("sync", lambda e: e.wait_ge(s_out, 16 * n_out[0]))

        with nc.Block() as block:
            @block.sync
            def _(eng):
                sc.replay("sync", eng)

            @block.vector
            def _(eng):
                sc.replay("vector", eng)

            @block.scalar
            def _(eng):
                sc.replay("scalar", eng)

            @block.tensor
            def _(eng):
                sc.replay("tensor", eng)
    return nc



def build_l2(phases=3, debug=False):
    nc = bass.Bass("TRN2", target_bir_lowering=False)
    xr_h = nc.dram_tensor("xrows", [TOK, D], F32, kind="ExternalInput").ap()
    yr_h = nc.dram_tensor("yarows", [TOK, D], F32, kind="ExternalInput").ap()
    wt_h = nc.dram_tensor("WTb", [32, 128, KD, 128], F32, kind="ExternalInput").ap()
    bb_h = nc.dram_tensor("bb", [128, 32], F32, kind="ExternalInput").ap()
    lA_h = nc.dram_tensor("loraAT", [D, 128], F32, kind="ExternalInput").ap()
    lB_h = nc.dram_tensor("loraBT", [128, D], F32, kind="ExternalInput").ap()
    rw_h = nc.dram_tensor("rwT", [D, 16], F32, kind="ExternalInput").ap()
    rb_h = nc.dram_tensor("rb", [16, 1], F32, kind="ExternalInput").ap()
    pr_h = nc.dram_tensor("prior", [16, 1], F32, kind="ExternalInput").ap()
    ac_h = nc.dram_tensor("act4", [4, 1], I32, kind="ExternalInput").ap()
    acr_h = nc.dram_tensor("actrow", [1, 4], I32, kind="ExternalInput").ap()
    io16_h = nc.dram_tensor("iota16", [16, 1], F32, kind="ExternalInput").ap()
    ior_h = nc.dram_tensor("iotar", [1, 16], F32, kind="ExternalInput").ap()
    idf_h = nc.dram_tensor("identf", [128, 128], F32, kind="ExternalInput").ap()
    yo_h = nc.dram_tensor("youtT", [D, TOK], F32, kind="ExternalOutput").ap()
    x1sc = nc.dram_tensor("x1T_sc", [D, TOK], F32).ap()
    gate_sc = nc.dram_tensor("gate_sc", [16, TOK // 2], F32).ap()
    if debug:
        dmp = {nm: nc.dram_tensor("dmp_" + nm, shp, F32, kind="ExternalOutput").ap()
               for nm, shp in [("x1", [128, D]), ("xt1", [128, D]), ("zall", [128, TOK // 2]),
                               ("logits", [16, TOK // 2]), ("lslot", [4, TOK // 2]),
                               ("atok", [128, (TOK // 2) // 128, 4]), ("gateT", [16, TOK // 2]),
                               ("wl", [128, TOK // 2]), ("gbc", [128, TOK // 2])]}

    HT = TOK // 2
    NTL = HT // 128

    with contextlib.ExitStack() as ctx:
        sbt = lambda n, s, d: ctx.enter_context(nc.sbuf_tensor(n, s, d))
        pst = lambda n, s, d: ctx.enter_context(nc.psum_tensor(n, s, d))
        sem = lambda n: ctx.enter_context(nc.semaphore(n))

        xrow = sbt("xrow", [128, D], F32)
        yarow = sbt("yarow", [128, D], F32)
        x1 = sbt("x1", [128, D], F32)
        xt1b = sbt("xt1b", [128, D], BF16)
        x1stg = sbt("x1stg", [128, 2, 128], F32)
        xT = sbt("xT", [128, KD, HT], BF16)
        wblk = sbt("wblk", [128, KD, 128], F32)
        wblkb = sbt("wblkb", [128, 2, KD, 128], BF16)
        stgf = sbt("stgf", [128, 128], F32)
        bbs = sbt("bbs", [128, 32], F32)
        lATb = sbt("lATb", [128, KD, 128], BF16)
        lBTb = sbt("lBTb", [128, D], BF16)
        rwTb = sbt("rwTb", [128, KD, 16], BF16)
        rb = sbt("rb_sb", [16, 1], F32)
        prior = sbt("prior_sb", [16, 1], F32)
        act4 = sbt("act4_sb", [4, 1], I32)
        actf = sbt("actf", [4, 1], F32)
        acri = sbt("acri", [1, 4], I32)
        actrow = sbt("actrow_sb", [1, 4], F32)
        ones16 = sbt("ones16", [1, 16], F32)
        io16 = sbt("io16_sb", [16, 1], F32)
        iorow = sbt("iorow", [4, 16], F32)
        oh164 = sbt("oh164", [16, 4], F32)
        oh46 = sbt("oh46", [4, 16], F32)
        acbc = sbt("acbc", [16, 4], F32)
        pslots = sbt("pslots", [4, 1], F32)
        sbias = sbt("sbias", [4, 1], F32)
        identb = sbt("identb_sb", [128, 128], BF16)
        identf = sbt("identf_sb", [128, 128], F32)
        tc = sbt("tc", [128, 4], F32)
        zall = sbt("zall", [128, HT], F32)
        gatebc = sbt("gatebc", [128, HT], F32)
        wlorab = sbt("wlorab", [128, HT], BF16)
        logits = sbt("logits", [16, HT], F32)
        lslot = sbt("lslot", [4, HT], F32)
        ltok = sbt("ltok", [128, NTL, 4], F32)
        atok = sbt("atok", [128, NTL, 4], F32)
        aslT = sbt("aslT", [4, HT], F32)
        gateT = sbt("gateT", [16, HT], F32)
        x1t = sbt("x1t", [128, 2, 512], F32)
        yo = sbt("yo", [128, 2, 512], F32)

        ptrb = [pst(f"ptrb{i}", [128, 128], BF16) for i in range(2)]
        ptrf = [pst(f"ptrf{i}", [128, 128], F32) for i in range(2)]
        pw = [pst(f"pw{i}", [128, 512], F32) for i in range(2)]
        pz = pst("pz", [128, 512], F32)
        plog = pst("plog", [16, 512], F32)

        sems = dict(d_in=sem("d_in"), s_act=sem("s_act"), s_dve=sem("s_dve"), s_pe=sem("s_pe"))
        d_w = sem("d_w")
        s_out = sem("s_out")
        s_x1 = sem("s_x1")
        sc = Sched(["sync", "vector", "scalar", "tensor"])
        dma, act, dve, pe, st = _mk_emitters(sc, sems)
        d_in, s_act, s_dve, s_pe = sems["d_in"], sems["s_act"], sems["s_dve"], sems["s_pe"]
        n_dw = [0]
        n_out = [0]
        n_x1 = [0]

        def dma_w(out, in_):
            n_dw[0] += 1
            sc.emit("scalar", lambda e, o=out, i=in_: e.dma_start(out=o, in_=i).then_inc(d_w, 16))

        def dma_out(out, in_):
            n_out[0] += 1
            sc.emit("sync", lambda e, o=out, i=in_: e.dma_start(out=o, in_=i).then_inc(s_out, 16))

        def dma_x1(out, in_):
            n_x1[0] += 1
            sc.emit("sync", lambda e, o=out, i=in_: e.dma_start(out=o, in_=i).then_inc(s_x1, 16))

        def dump(name, src_ap):
            if not debug:
                return
            sc.wait("sync", "v", s_dve, st["dve"])
            sc.wait("sync", "a", s_act, st["act"])
            dma_out(dmp[name], src_ap)
            sc.wait("vector", "o", s_out, 16 * n_out[0])
            sc.wait("scalar", "o", s_out, 16 * n_out[0])

        dma(bbs[:], bb_h)
        dma(rb[:], rb_h)
        dma(prior[:], pr_h)
        dma(act4[:], ac_h)
        dma(acri[:], acr_h)
        dma(io16[:], io16_h)
        dma(identf[:], idf_h)
        dma(iorow[:], ior_h.partition_broadcast(4))
        cast_waits = []
        for jc in range(KD):
            dma(stgf[:], lA_h.rearrange("(c p) m -> p c m", p=128)[:, jc, :])
            sc.wait("vector", "di", d_in, 16 * st["din"])
            dve(lambda e, j=jc: e.tensor_copy(out=lATb[:, j, :], in_=stgf[:]))
            cast_waits.append(st["dve"])
            sc.wait("sync", "v", s_dve, st["dve"])
        for jc in range(KD):
            dma(stgf[:], lB_h[:, jc * 128:(jc + 1) * 128])
            sc.wait("vector", "di", d_in, 16 * st["din"])
            dve(lambda e, j=jc: e.tensor_copy(out=lBTb[:, j * 128:(j + 1) * 128], in_=stgf[:, :]))
            sc.wait("sync", "v", s_dve, st["dve"])
        for jc in range(KD):
            dma(stgf[:, 0:16], rw_h.rearrange("(c p) m -> p c m", p=128)[:, jc, :])
            sc.wait("vector", "di", d_in, 16 * st["din"])
            dve(lambda e, j=jc: e.tensor_copy(out=rwTb[:, j, :], in_=stgf[:, 0:16]))
            sc.wait("sync", "v", s_dve, st["dve"])

        sc.wait("vector", "di", d_in, 16 * st["din"])
        dve(lambda e: e.tensor_copy(out=identb[:], in_=identf[:]))
        dve(lambda e: e.tensor_copy(out=actf[:], in_=act4[:]))
        dve(lambda e: e.tensor_copy(out=actrow[:], in_=acri[:]))
        dve(lambda e: e.memset(ones16[:], 1.0))
        sc.wait("vector", "v", s_dve, st["dve"])
        sc.wait("tensor", "v", s_dve, st["dve"])
        pe(lambda e: e.matmul(plog[0:16, 0:4], ones16[:], actrow[:], start=True, stop=True))
        sc.wait("vector", "p", s_pe, st["pe"])
        dve(lambda e: e.tensor_copy(out=acbc[:], in_=plog[0:16, 0:4]))
        sc.wait("vector", "v", s_dve, st["dve"])
        dve(lambda e: e.tensor_tensor(out=oh164[:], in0=io16.broadcast_to([16, 4]), in1=acbc[:], op=ALU.is_equal))
        sc.wait("vector", "v", s_dve, st["dve"])
        dve(lambda e: e.tensor_tensor(out=oh46[:], in0=actf.broadcast_to([4, 16]), in1=iorow[:], op=ALU.is_equal))
        sc.wait("vector", "v", s_dve, st["dve"])
        sc.wait("tensor", "v", s_dve, st["dve"])
        pe(lambda e: e.matmul(plog[0:4, 0:1], oh164[:], prior[:], start=True, stop=True))
        sc.wait("vector", "p", s_pe, st["pe"])
        dve(lambda e: e.tensor_copy(out=pslots[:], in_=plog[0:4, 0:1]))
        sc.wait("vector", "v", s_dve, st["dve"])
        sc.wait("tensor", "v", s_dve, st["dve"])
        pe(lambda e: e.matmul(plog[0:4, 0:1], oh164[:], rb[:], start=True, stop=True))
        dve(lambda e: e.tensor_scalar_add(out=pslots[:], in0=pslots[:], scalar1=EPS_R))
        sc.wait("scalar", "v", s_dve, st["dve"])
        act(lambda e: e.activation(out=pslots[:], in_=pslots[:], func=AF.Ln))
        sc.wait("vector", "p", s_pe, st["pe"])
        sc.wait("vector", "a", s_act, st["act"])
        dve(lambda e: e.tensor_add(sbias[:], pslots[:], plog[0:4, 0:1]))
        sc.wait("vector", "v", s_dve, st["dve"])
        prolog_dve = st["dve"]

        ev_dve, ln2_act, tr_pe, sq_dve = {}, {}, {}, {}
        evac_act, yo_dve, out_at, wcast_dve = {}, {}, {}, {}

        for hf in range(2):
            base_tok = hf * HT
            for i in range(NTL):
                gi = hf * NTL + i
                row0 = base_tok + i * 128
                sc.wait("sync", "v", s_dve, ln2_act.get(gi - 1))
                sc.wait("sync", "v", s_dve, sq_dve.get(gi - 1))
                dma(xrow[:], xr_h[row0:row0 + 128, :])
                dma(yarow[:], yr_h[row0:row0 + 128, :])
                c0, c1, c2 = tc[:, 0:1], tc[:, 1:2], tc[:, 2:3]
                sc.wait("vector", "di", d_in, 16 * st["din"])
                dve(lambda e: e.tensor_add(x1[:], xrow[:], yarow[:]))
                sc.wait("vector", "v", s_dve, st["dve"])
                dve(lambda e, c=c0: e.reduce_sum(out=c, in_=x1[:], axis=AX.X))
                sc.wait("vector", "v", s_dve, st["dve"])
                dve(lambda e, c=c0: e.tensor_scalar_mul(out=c, in0=c, scalar1=1.0 / D))
                sc.wait("scalar", "v", s_dve, st["dve"])
                act(lambda e: e.activation(out=yarow[:], in_=x1[:], func=AF.Square))
                sc.wait("vector", "a", s_act, st["act"])
                dve(lambda e, c=c1: e.reduce_sum(out=c, in_=yarow[:], axis=AX.X))
                sq_dve[gi] = st["dve"]
                sc.wait("vector", "v", s_dve, st["dve"])
                dve(lambda e, c=c1: e.tensor_scalar_mul(out=c, in0=c, scalar1=1.0 / D))
                sc.wait("vector", "v", s_dve, st["dve"])
                dve(lambda e, a=c0, c=c2: e.tensor_mul(c, a, a))
                sc.wait("vector", "v", s_dve, st["dve"])
                dve(lambda e, a=c1, c=c2: e.tensor_sub(a, a, c))
                sc.wait("vector", "v", s_dve, st["dve"])
                dve(lambda e, c=c1: e.tensor_scalar_add(out=c, in0=c, scalar1=EPS_LN))
                sc.wait("scalar", "v", s_dve, st["dve"])
                act(lambda e, c=c1: e.activation(out=c, in_=c, func=AF.Ln))
                sc.wait("vector", "a", s_act, st["act"])
                dve(lambda e, c=c1: e.tensor_scalar_mul(out=c, in0=c, scalar1=-0.5))
                sc.wait("scalar", "v", s_dve, st["dve"])
                act(lambda e, c=c1: e.activation(out=c, in_=c, func=AF.Exp))
                sc.wait("vector", "a", s_act, st["act"])
                dve(lambda e, a=c0, b_=c1, c=c2: e.tensor_mul(c, a, b_))
                sc.wait("vector", "v", s_dve, st["dve"])
                dve(lambda e, c=c2: e.tensor_scalar_mul(out=c, in0=c, scalar1=-1.0))
                sc.wait("vector", "v", s_dve, st["dve"])
                sc.wait("vector", "p", s_pe, tr_pe.get(gi - 1))
                dve(lambda e, r=c1, m=c2: e.tensor_scalar(out=xt1b[:], in0=x1[:], scalar1=r, scalar2=m, op0=ALU.mult, op1=ALU.add))
                ln2_act[gi] = st["dve"]
                if debug and gi == 0:
                    dump("x1", x1[:])
                    dve(lambda e: e.tensor_copy(out=wblk[:].rearrange("p a b -> p (a b)"), in_=xt1b[:]))
                    sc.wait("vector", "v", s_dve, st["dve"])
                    dump("xt1", wblk[:].rearrange("p a b -> p (a b)"))
                for j in range(KD):
                    pj = j % 2
                    sc.wait("tensor", "v", s_dve, ln2_act[gi])
                    prev = ev_dve.get((gi, j - 2)) if j >= 2 else ev_dve.get((gi - 1, 2 * KD - 2 + j))
                    sc.wait("tensor", "v", s_dve, prev)
                    pe(lambda e, j_=j, pj_=pj: e.transpose(ptrb[pj_][:], xt1b[:, j_ * 128:(j_ + 1) * 128], identb[:]))
                    sc.wait("vector", "p", s_pe, st["pe"])
                    dve(lambda e, j_=j, pj_=pj, i_=i: e.tensor_copy(out=xT[:, j_, i_ * 128:(i_ + 1) * 128], in_=ptrb[pj_][:]))
                    ev_dve[(gi, j)] = st["dve"]
                for j in range(KD):
                    pj = j % 2
                    sc.wait("tensor", "v", s_dve, ev_dve.get((gi, KD + j - 2)) if j >= 2 else None)
                    pe(lambda e, j_=j, pj_=pj: e.transpose(ptrf[pj_][:], x1[:, j_ * 128:(j_ + 1) * 128], identf[:]))
                    sc.wait("vector", "p", s_pe, st["pe"])
                    sc.wait("vector", "x1", s_x1, 16 * (n_x1[0] - 1) if n_x1[0] >= 1 else None)
                    dve(lambda e, j_=j, pj_=pj: e.tensor_copy(out=x1stg[:, j_ % 2, :], in_=ptrf[pj_][:]))
                    ev_dve[(gi, KD + j)] = st["dve"]
                    sc.wait("sync", "v", s_dve, st["dve"])
                    dma_x1(x1sc[j * 128:(j + 1) * 128, row0:row0 + 128], x1stg[:, j % 2, :])
                tr_pe[gi] = st["pe"]

            ph1_dve = st["dve"]
            if phases < 1.1:
                continue

            for lc2 in range(HT // 512):
                sc.wait("tensor", "v", s_dve, max(ph1_dve, st["dve"]))
                for kc in range(KD):
                    pe(lambda e, k=kc, l2=lc2: e.matmul(pz[:], lATb[:, k, :], xT[:, k, l2 * 512:(l2 + 1) * 512], start=(k == 0), stop=(k == KD - 1)))
                sc.wait("vector", "p", s_pe, st["pe"])
                dve(lambda e, l2=lc2: e.tensor_copy(out=zall[:, l2 * 512:(l2 + 1) * 512], in_=pz[:]))
            if phases < 1.4:
                continue
            sc.wait("tensor", "v", s_dve, prolog_dve)
            for lc2 in range(HT // 512):
                sc.wait("tensor", "v", s_dve, st["dve"])
                for kc in range(KD):
                    pe(lambda e, k=kc, l2=lc2: e.matmul(plog[:], rwTb[:, k, :], xT[:, k, l2 * 512:(l2 + 1) * 512], start=(k == 0), stop=(k == KD - 1)))
                sc.wait("vector", "p", s_pe, st["pe"])
                dve(lambda e, l2=lc2: e.tensor_scalar_add(out=logits[:, l2 * 512:(l2 + 1) * 512], in0=plog[:], scalar1=rb[:]))
            if phases < 1.6:
                continue
            sc.wait("tensor", "v", s_dve, st["dve"])
            for lc2 in range(HT // 512):
                sc.wait("tensor", "v", s_dve, st["dve"])
                pe(lambda e, l2=lc2: e.matmul(plog[0:4, :], oh164[:], logits[:, l2 * 512:(l2 + 1) * 512], start=True, stop=True))
                sc.wait("vector", "p", s_pe, st["pe"])
                dve(lambda e, l2=lc2: e.tensor_scalar_add(out=lslot[:, l2 * 512:(l2 + 1) * 512], in0=plog[0:4, :], scalar1=sbias[:]))
            sc.wait("tensor", "v", s_dve, st["dve"])
            for i in range(NTL):
                pj = i % 2
                sc.wait("tensor", "v", s_dve, ev_dve.get(("lt", hf, i - 2)) if i >= 2 else st["dve"])
                pe(lambda e, i_=i, pj_=pj: e.transpose(ptrf[pj_][0:128, 0:4], lslot[:, i_ * 128:(i_ + 1) * 128], identf[0:4, 0:4]))
                sc.wait("vector", "p", s_pe, st["pe"])
                dve(lambda e, i_=i, pj_=pj: e.tensor_copy(out=ltok[:, i_, :], in_=ptrf[pj_][0:128, 0:4]))
                ev_dve[("lt", hf, i)] = st["dve"]
            if phases < 1.8:
                continue
            c0, c1 = tc[:, 0:1], tc[:, 1:2]
            for i in range(NTL):
                dve(lambda e, i_=i, c=c0: e.reduce_max(out=c, in_=ltok[:, i_, :], axis=AX.X))
                sc.wait("vector", "v", s_dve, st["dve"])
                dve(lambda e, c=c0: e.tensor_scalar_mul(out=c, in0=c, scalar1=-1.0))
                sc.wait("scalar", "v", s_dve, st["dve"])
                act(lambda e, i_=i, c=c0: e.activation(out=atok[:, i_, :], in_=ltok[:, i_, :], func=AF.Exp, bias=c))
                sc.wait("vector", "a", s_act, st["act"])
                dve(lambda e, i_=i, c=c1: e.reduce_sum(out=c, in_=atok[:, i_, :], axis=AX.X))
                sc.wait("vector", "v", s_dve, st["dve"])
                dve(lambda e, c=c1: e.reciprocal(out=c, in_=c))
                sc.wait("vector", "v", s_dve, st["dve"])
                dve(lambda e, i_=i, c=c1: e.tensor_scalar(out=atok[:, i_, :], in0=atok[:, i_, :], scalar1=c, scalar2=None, op0=ALU.mult))
            for i in range(NTL):
                pj = i % 2
                sc.wait("tensor", "v", s_dve, ev_dve.get(("at", hf, i - 2)) if i >= 2 else st["dve"])
                pe(lambda e, i_=i, pj_=pj: e.transpose(ptrf[pj_][0:4, 0:128], atok[:, i_, :], identf[:]))
                sc.wait("vector", "p", s_pe, st["pe"])
                dve(lambda e, i_=i, pj_=pj: e.tensor_copy(out=aslT[:, i_ * 128:(i_ + 1) * 128], in_=ptrf[pj_][0:4, 0:128]))
                ev_dve[("at", hf, i)] = st["dve"]
            if phases < 1.9:
                continue
            sc.wait("tensor", "v", s_dve, st["dve"])
            for lc2 in range(HT // 512):
                sc.wait("tensor", "v", s_dve, st["dve"])
                pe(lambda e, l2=lc2: e.matmul(plog[:, :], oh46[:], aslT[:, l2 * 512:(l2 + 1) * 512], start=True, stop=True))
                sc.wait("vector", "p", s_pe, st["pe"])
                dve(lambda e, l2=lc2: e.tensor_copy(out=gateT[:, l2 * 512:(l2 + 1) * 512], in_=plog[:]))
            sc.wait("sync", "v", s_dve, st["dve"])
            dma(gate_sc, gateT[:])
            sc.wait("sync", "di", d_in, 16 * st["din"])
            for e16 in range(16):
                dma(gatebc[e16 * 8:(e16 + 1) * 8, :], gate_sc[e16:e16 + 1, :].partition_broadcast(8))
            sc.wait("vector", "di", d_in, 16 * st["din"])
            if phases < 1.95:
                continue
            if debug:
                dump("zall", zall[:])
                dump("logits", logits[:])
                dump("lslot", lslot[:])
                dump("atok", atok[:])
                dump("gateT", gateT[:])
                dump("gbc", gatebc[:])
            dve(lambda e: e.scalar_tensor_tensor(out=wlorab[:], in0=zall[:], scalar=SCALING, in1=gatebc[:], op0=ALU.mult, op1=ALU.mult))
            lora_dve = st["dve"]
            if debug:
                dve(lambda e: e.tensor_copy(out=zall[:], in_=wlorab[:]))
                sc.wait("vector", "v", s_dve, st["dve"])
                dump("wl", zall[:])
                dve(lambda e: e.scalar_tensor_tensor(out=zall[:], in0=wlorab[:], scalar=1.0, in1=wlorab[:], op0=ALU.mult, op1=ALU.bypass))
            if phases < 3:
                continue

            for ot in range(32):
                po = ot % 2
                gidx0 = hf * 64 + ot * 2
                sc.wait("scalar", "v", s_dve, wcast_dve.get(hf * 32 + ot - 1))
                dma_w(wblk[:], wt_h[ot])
                sc.wait("vector", "dw", d_w, 16 * n_dw[0])
                sc.wait("vector", "p", s_pe, tr_pe.get(hf * NTL + NTL - 1) if ot < 2 else out_at.get(("pe", gidx0 - 3)))
                dve(lambda e, p=po: e.tensor_copy(out=wblkb[:, p, :, :], in_=wblk[:]))
                wcast_dve[hf * 32 + ot] = st["dve"]
                for lc2 in range(2):
                    gidx = gidx0 + lc2
                    pp = gidx % 2
                    sc.wait("tensor", "v", s_dve, max(wcast_dve[hf * 32 + ot], lora_dve, yo_dve.get(gidx - 2) or 0))
                    for kc in range(KD):
                        pe(lambda e, k=kc, p=po, l2=lc2, q=pp: e.matmul(pw[q][:], wblkb[:, p, k, :], xT[:, k, l2 * 512:(l2 + 1) * 512], start=(k == 0), stop=False))
                    pe(lambda e, l2=lc2, q=pp, o=ot: e.matmul(pw[q][:], lBTb[:, o * 128:(o + 1) * 128], wlorab[:, l2 * 512:(l2 + 1) * 512], start=False, stop=True))
                    out_at[("pe", gidx)] = st["pe"]
                    sc.wait("sync", "x1", s_x1, 16 * n_x1[0])
                    sc.wait("sync", "v", s_dve, yo_dve.get(gidx - 2))
                    dma(x1t[:, pp, :], x1sc[ot * 128:(ot + 1) * 128, base_tok + lc2 * 512: base_tok + (lc2 + 1) * 512])
                    sc.wait("vector", "di", d_in, 16 * st["din"])
                    sc.wait("vector", "p", s_pe, st["pe"])
                    oc = out_at.get(("out", gidx - 2))
                    sc.wait("vector", "o", s_out, 16 * oc if oc else None)
                    dve(lambda e, q=pp, o=ot: e.scalar_tensor_tensor(out=yo[:, q, :], in0=pw[q][:], scalar=bbs[:, o:o + 1], in1=x1t[:, q, :], op0=ALU.add, op1=ALU.add))
                    yo_dve[gidx] = st["dve"]
                    sc.wait("sync", "v", s_dve, yo_dve[gidx])
                    dma_out(yo_h[ot * 128:(ot + 1) * 128, base_tok + lc2 * 512: base_tok + (lc2 + 1) * 512], yo[:, pp, :])
                    out_at[("out", gidx)] = n_out[0]

            if hf == 0:
                sc.wait("vector", "p", s_pe, st["pe"])
                sc.wait("scalar", "p", s_pe, st["pe"])

        if phases >= 3:
            sc.emit("sync", lambda e: e.wait_ge(s_out, 16 * n_out[0]))
        else:
            sc.wait("sync", "x1f", s_x1, 16 * n_x1[0])
            sc.wait("sync", "v", s_dve, st["dve"])
            sc.wait("sync", "a", s_act, st["act"])
            sc.wait("sync", "p", s_pe, st["pe"])

        with nc.Block() as block:
            @block.sync
            def _(eng):
                sc.replay("sync", eng)

            @block.vector
            def _(eng):
                sc.replay("vector", eng)

            @block.scalar
            def _(eng):
                sc.replay("scalar", eng)

            @block.tensor
            def _(eng):
                sc.replay("tensor", eng)
    return nc


_CACHE = {}


def _programs():
    if "l0" not in _CACHE:
        _CACHE["l0"] = build_l0()
        _CACHE["l1"] = build_l1()
        _CACHE["l2"] = build_l2()
    return _CACHE["l0"], _CACHE["l1"], _CACHE["l2"]


def kernel(L0_attn_x, L0_attn_A_log, L0_attn_D, L0_attn_x_proj_w, L0_attn_dt_proj_w,
           L0_attn_dt_proj_b, L0_ffn_W_base, L0_ffn_b_base, L0_ffn_lora_A, L0_ffn_lora_B,
           L0_ffn_router_W, L0_ffn_router_b, L0_ffn_domain_prior, L0_ffn_active):
    x = np.ascontiguousarray(np.asarray(L0_attn_x, dtype=np.float32))
    x2d = x.reshape(B * L, D)
    f32 = lambda a: np.ascontiguousarray(np.asarray(a, dtype=np.float32))
    identf = np.eye(128, dtype=np.float32)

    nc0, nc1, nc2 = _programs()

    xpwT = np.ascontiguousarray(f32(L0_attn_x_proj_w).T)
    in0 = [dict(xrows=x2d[k * TOK:(k + 1) * TOK], xpwT=xpwT, identf=identf) for k in range(NCORES)]
    r0 = run_bass_kernel_spmd(nc0, in0, core_ids=list(range(NCORES))).results
    stats_all = np.concatenate(
        [np.asarray(r0[k]["stats"]).transpose(1, 0, 2).reshape(TOK, 2) for k in range(NCORES)], axis=0)
    xp_all = np.concatenate(
        [np.concatenate([r0[k]["xp_hi"], r0[k]["xp_lo"]], axis=0) for k in range(NCORES)], axis=1)

    dtwT = np.ascontiguousarray(f32(L0_attn_dt_proj_w).T)
    dtb = f32(L0_attn_dt_proj_b)
    alog = f32(L0_attn_A_log)
    dvec = f32(L0_attn_D)
    in1 = []
    for core in range(NCORES):
        b, q = core // 4, core % 4
        qs = slice(q * DQ, (q + 1) * DQ)
        in1.append(dict(
            xqT=np.ascontiguousarray(x[b].T[qs]),
            statsT=np.ascontiguousarray(stats_all[b * L:(b + 1) * L].T),
            xpT=np.ascontiguousarray(xp_all[:, b * L:(b + 1) * L]),
            dtwT=np.ascontiguousarray(dtwT[:, qs]),
            dtb=np.ascontiguousarray(dtb[qs].reshape(NT, 128).T),
            alog=np.ascontiguousarray(alog[qs].reshape(NT, 128, NS).transpose(1, 0, 2)),
            dvec=np.ascontiguousarray(dvec[qs].reshape(NT, 128).T),
        ))
    r1 = run_bass_kernel_spmd(nc1, in1, core_ids=list(range(NCORES))).results
    ya = np.empty((B * L, D), dtype=np.float32)
    for b in range(B):
        yab = np.concatenate([r1[b * 4 + q]["ya"] for q in range(4)], axis=0)
        ya[b * L:(b + 1) * L] = yab.T

    WTb = np.ascontiguousarray(
        f32(L0_ffn_W_base).T.reshape(KD, 128, 32, 128).transpose(2, 1, 0, 3))
    bb = np.ascontiguousarray(f32(L0_ffn_b_base).reshape(32, 128).T)
    loraAT = np.ascontiguousarray(f32(L0_ffn_lora_A).reshape(128, D).T)
    loraBT = np.ascontiguousarray(f32(L0_ffn_lora_B).transpose(0, 2, 1).reshape(128, D))
    rwT = np.ascontiguousarray(f32(L0_ffn_router_W).T)
    rb = f32(L0_ffn_router_b).reshape(16, 1)
    prior = f32(L0_ffn_domain_prior).reshape(16, 1)
    act4 = np.ascontiguousarray(np.asarray(L0_ffn_active, dtype=np.int32).reshape(4, 1))
    actrow = np.ascontiguousarray(np.asarray(L0_ffn_active, dtype=np.int32).reshape(1, 4))
    io16 = np.arange(16, dtype=np.float32).reshape(16, 1)
    iotar = np.arange(16, dtype=np.float32).reshape(1, 16)
    in2 = []
    for k in range(NCORES):
        rs = slice(k * TOK, (k + 1) * TOK)
        in2.append(dict(
            xrows=x2d[rs], yarows=ya[rs], WTb=WTb, bb=bb, loraAT=loraAT, loraBT=loraBT,
            rwT=rwT, rb=rb, prior=prior, act4=act4, actrow=actrow, iota16=io16, iotar=iotar,
            identf=identf,
        ))
    r2 = run_bass_kernel_spmd(nc2, in2, core_ids=list(range(NCORES))).results
    out = np.empty((B * L, D), dtype=np.float32)
    for k in range(NCORES):
        out[k * TOK:(k + 1) * TOK] = r2[k]["youtT"].T
    return out.reshape(B, L, D)
